# version 01 (final)
# baseline (speedup 1.0000x reference)
import contextlib
import numpy as np
import ml_dtypes
import concourse.bass as bass
import concourse.mybir as mybir
from concourse.bass_utils import run_bass_kernel_spmd

F32 = mybir.dt.float32
BF16 = mybir.dt.bfloat16
AF = mybir.ActivationFunctionType
ALU = mybir.AluOpType
AX = mybir.AxisListType

NCORES = 8
T = 1024
D = 2048
KCN = 16
H = 16
NEG = -1.0e30
EPS = 1e-5
CSC = float(1.0 / np.sqrt(128.0))
INTER0 = 5632
IC0 = 44
E1 = 8
INTER1 = 1408
IC1 = 11

ENGINES = ("sync", "vector", "scalar", "tensor", "gpsimd")
BF = ml_dtypes.bfloat16


class Prog:
    def __init__(self):
        self.ops = {e: [] for e in ENGINES}
        self.sc = {}
        self.sems = {}

    def sem(self, name):
        self.sc[name] = 0

    def emit(self, eng, fn):
        self.ops[eng].append(fn)

    def wait(self, eng, sem, cnt):
        if cnt is None or cnt <= 0:
            return
        h = self.sems
        self.emit(eng, lambda E, s=sem, c=cnt: E.wait_ge(h[s], c))

    def op(self, eng, fn, inc=None, amt=1):
        if inc is None:
            self.emit(eng, fn)
            return None
        self.sc[inc] += amt
        cnt = self.sc[inc]
        h = self.sems
        self.emit(eng, lambda E, f=fn, s=inc, a=amt: f(E).then_inc(h[s], a))
        return cnt

    def dma(self, eng, sem, dst, src):
        return self.op(eng, lambda E, d=dst, s=src: E.dma_start(out=d, in_=s),
                       inc=sem, amt=16)


class Banks:
    def __init__(self, prog):
        self.p = prog
        self.state = {i: None for i in range(8)}

    def acquire(self, i):
        st = self.state[i]
        if st is not None:
            self.p.wait("tensor", st[0], st[1])
        self.state[i] = None

    def release(self, i, sem, cnt):
        self.state[i] = (sem, cnt)


class Arena:
    def __init__(self, handle, nbytes):
        self.h = handle
        self.nbytes = nbytes
        self.off = 0

    def reset(self, off=0):
        self.off = off

    def alloc(self, nbytes, align=64):
        o = (self.off + align - 1) // align * align
        self.off = o + nbytes
        assert self.off <= self.nbytes, f"arena overflow: {self.off} > {self.nbytes}"
        return o

    def f32(self, off, cols, rows=128, row0=0):
        assert off % 4 == 0
        return self.h[row0:row0 + rows, off // 4: off // 4 + cols]

    def bf(self, off, cols, rows=128, row0=0):
        assert off % 2 == 0
        return self.h[:].bitcast(BF16)[row0:row0 + rows, off // 2: off // 2 + cols]


DBG_ATTN = False


def build_program(upto="all"):
    nc = bass.Bass("TRN2", target_bir_lowering=False)

    xio = nc.dram_tensor("x", [D, T], F32, kind="ExternalInput").ap()
    xout = nc.dram_tensor("xout", [D, T], F32, kind="ExternalOutput").ap()

    wqkv = [nc.dram_tensor(f"wqkv{l}", [D, 3 * D], BF16, kind="ExternalInput").ap() for l in (0, 1)]
    wout = [nc.dram_tensor(f"wout{l}", [D, D], BF16, kind="ExternalInput").ap() for l in (0, 1)]
    gwTd = nc.dram_tensor("gwT0", [D, INTER0], BF16, kind="ExternalInput").ap()
    uwTd = nc.dram_tensor("uwT0", [D, INTER0], BF16, kind="ExternalInput").ap()
    dwTd = nc.dram_tensor("dwT0", [INTER0, D], BF16, kind="ExternalInput").ap()
    mgw = nc.dram_tensor("mgw", [E1, D, INTER1], BF16, kind="ExternalInput")
    muw = nc.dram_tensor("muw", [E1, D, INTER1], BF16, kind="ExternalInput")
    mdw = nc.dram_tensor("mdw", [E1, INTER1, D], BF16, kind="ExternalInput")

    c_id = nc.dram_tensor("c_id", [128, 128], BF16, kind="ExternalInput").ap()
    c_idf = nc.dram_tensor("c_idf", [128, 128], F32, kind="ExternalInput").ap()
    c_mask = nc.dram_tensor("c_mask", [128, 1024], BF16, kind="ExternalInput").ap()
    c_iota = nc.dram_tensor("c_iota", [128, 8], F32, kind="ExternalInput").ap()
    c_onesbf = nc.dram_tensor("c_onesbf", [128, 1], BF16, kind="ExternalInput").ap()
    c_onesf = nc.dram_tensor("c_onesf", [1, 128], F32, kind="ExternalInput").ap()
    c_gs = nc.dram_tensor("c_gs", [128, IC0], F32, kind="ExternalInput").ap()
    c_us = nc.dram_tensor("c_us", [128, IC0], F32, kind="ExternalInput").ap()
    c_ds = nc.dram_tensor("c_ds", [128, KCN], F32, kind="ExternalInput").ap()
    c_bq = nc.dram_tensor("c_bq", [128, KCN], F32, kind="ExternalInput").ap()
    c_bv = nc.dram_tensor("c_bv", [128, KCN], F32, kind="ExternalInput").ap()
    c_bp = nc.dram_tensor("c_bp", [128, KCN], F32, kind="ExternalInput").ap()
    c_rwt = nc.dram_tensor("c_rwt", [128, KCN * E1], F32, kind="ExternalInput").ap()

    dbg_xbf = nc.dram_tensor("dbg_xbf", [D, T], BF16, kind="ExternalOutput").ap()
    dbg_row = nc.dram_tensor("dbg_row", [2, 1024], F32, kind="ExternalOutput").ap()
    dbg_qt = nc.dram_tensor("dbg_qt", [D, T], BF16, kind="ExternalOutput").ap()
    dbg_p = nc.dram_tensor("dbg_p", [128, 10240], BF16, kind="ExternalOutput").ap()
    dbg_sst = nc.dram_tensor("dbg_sst", [128, 4096], F32, kind="ExternalOutput").ap()
    dbg_st2 = nc.dram_tensor("dbg_st2", [128, 16], F32, kind="ExternalOutput").ap()
    dbg_kb = nc.dram_tensor("dbg_kb", [128, 4096], BF16, kind="ExternalOutput").ap()
    dbg_vb = nc.dram_tensor("dbg_vb", [128, 4096], BF16, kind="ExternalOutput").ap()
    dbg_ath = nc.dram_tensor("dbg_ath", [128, 512], BF16, kind="ExternalOutput").ap()
    dbg_selt = nc.dram_tensor("dbg_selt", [8, 1024], F32, kind="ExternalOutput").ap()
    dbg_wb = nc.dram_tensor("dbg_wb", [16, 1024], F32, kind="ExternalOutput").ap()
    dbg_ccin = nc.dram_tensor("dbg_ccin", [2 * D, T], BF16, kind="ExternalOutput").ap()
    dbg_ccout = nc.dram_tensor("dbg_ccout", [2 * 2 * D, T], BF16, kind="ExternalOutput").ap()
    cc_in = [nc.dram_tensor(f"cc_in{l}", [2 * D, T], BF16) for l in (0, 1)]
    rstd_scr = nc.dram_tensor("rstd_scr", [1, 1024], F32)
    cc_out = [nc.dram_tensor(f"cc_out{l}", [NCORES * 2 * D, T], BF16) for l in (0, 1)]

    prog = Prog()
    for s in ("ld", "kv", "gd", "pe", "dv", "ac", "cc0", "cc1"):
        prog.sem(s)
    banks = Banks(prog)

    ARN = 33500

    es = contextlib.ExitStack()
    with es:
        XT = es.enter_context(nc.sbuf_tensor("XT", [128, KCN * T], F32))
        ARH = es.enter_context(nc.sbuf_tensor("AR", [128, ARN], F32))
        AR = Arena(ARH, ARN * 4)
        S_ID = es.enter_context(nc.sbuf_tensor("s_id", [128, 128], BF16))
        S_IDF = es.enter_context(nc.sbuf_tensor("s_idf", [128, 128], F32))
        S_MASK = es.enter_context(nc.sbuf_tensor("s_mask", [128, 1024], BF16))
        S_IOTA = es.enter_context(nc.sbuf_tensor("s_iota", [128, 8], F32))
        S_ONESBF = es.enter_context(nc.sbuf_tensor("s_onesbf", [128, 1], BF16))
        S_ONESF = es.enter_context(nc.sbuf_tensor("s_onesf", [1, 128], F32))
        S_GS = es.enter_context(nc.sbuf_tensor("s_gs", [128, IC0], F32))
        S_US = es.enter_context(nc.sbuf_tensor("s_us", [128, IC0], F32))
        S_DS = es.enter_context(nc.sbuf_tensor("s_ds", [128, KCN], F32))
        S_BQ = es.enter_context(nc.sbuf_tensor("s_bq", [128, KCN], F32))
        S_BV = es.enter_context(nc.sbuf_tensor("s_bv", [128, KCN], F32))
        S_BP = es.enter_context(nc.sbuf_tensor("s_bp", [128, KCN], F32))
        S_RWT = es.enter_context(nc.sbuf_tensor("s_rwt", [128, KCN * E1], F32))
        S_ROWS = es.enter_context(nc.sbuf_tensor("s_rows", [128, 2048], F32))
        PS = [es.enter_context(nc.psum_tensor(f"ps{i}", [128, 512], F32)) for i in range(8)]
        for s in list(prog.sc):
            prog.sems[s] = es.enter_context(nc.semaphore(s))

        def psf(i):
            return PS[i][:]

        def psbf(i):
            return PS[i][:].bitcast(BF16)

        MR = S_ROWS[0:1, 0:1024]
        RSTDR = S_ROWS[0:1, 1024:2048]
        RSTDC = S_ROWS[:, 2048 - 8:2048]
        RSTDC = None

        S_RC = es.enter_context(nc.sbuf_tensor("s_rc", [128, 16], F32))
        RSTDC = S_RC[:, 0:8]
        PRIME_IN = S_RC[:, 8:9]
        PRIME_OUT = S_RC[:, 9:10]

        ld_x = prog.dma("sync", "ld",
                        XT[:].rearrange("p (kc t) -> p kc t", kc=KCN),
                        xio.rearrange("(kc p) t -> p kc t", p=128))
        ld_consts = ld_x
        for dst, src in ((S_ID[:], c_id), (S_IDF[:], c_idf), (S_MASK[:], c_mask),
                         (S_IOTA[:], c_iota), (S_ONESBF[:], c_onesbf), (S_ONESF[:], c_onesf),
                         (S_GS[:], c_gs), (S_US[:], c_us), (S_DS[:], c_ds),
                         (S_BQ[:], c_bq), (S_BV[:], c_bv), (S_BP[:], c_bp),
                         (S_RWT[:], c_rwt)):
            ld_consts = prog.dma("sync", "ld", dst, src)

        prog.wait("scalar", "ld", ld_consts)
        prog.op("scalar", lambda E: E.activation(out=PRIME_IN, in_=S_DS[:, 0:1],
                                                 func=AF.Exp, accum_out=PRIME_OUT), inc="ac")

        def emit_ln(prev_dv):
            AR.reset()
            xbf_o = AR.alloc(2 * KCN * T)
            xsq_o = AR.alloc(2 * KCN * T)
            row_o = AR.alloc(4 * 4096)
            scr_o = AR.alloc(4 * 512)
            XBF = AR.bf(xbf_o, KCN * T)
            XSQ = AR.bf(xsq_o, KCN * T)
            SUMR = AR.f32(row_o, 1024, rows=1)
            SQR = AR.f32(row_o + 4096, 1024, rows=1)
            VARR = AR.f32(row_o + 8192, 1024, rows=1)
            STDR = AR.f32(row_o + 12288, 1024, rows=1)
            SCR = AR.f32(scr_o, 512)

            prog.wait("vector", "ld", ld_x)
            c1 = prog.op("vector", lambda E: E.tensor_copy(out=XBF, in_=XT[:]), inc="dv")
            prog.wait("scalar", "ld", ld_x)
            prog.wait("scalar", "dv", prev_dv)
            c2 = prog.op("scalar", lambda E: E.activation(out=XSQ, in_=XT[:], func=AF.Square),
                         inc="ac")
            prog.wait("tensor", "dv", c1)
            prog.wait("tensor", "ac", c2)

            for half in (0, 1):
                pecnt = {}
                for q in (0, 1):
                    bank = q
                    banks.acquire(bank)
                    src = XBF if q == 0 else XSQ
                    for kc in range(KCN):
                        rhs = src[:, kc * T + half * 512: kc * T + half * 512 + 512]
                        pecnt[q] = prog.op(
                            "tensor",
                            lambda E, o=PS[bank][0:1, 0:512], r=rhs, st=(kc == 0), sp=(kc == KCN - 1):
                            E.matmul(out=o, lhsT=S_ONESBF[:], rhs=r, start=st, stop=sp),
                            inc=("pe" if kc == KCN - 1 else None))
                for q, dstrow in ((0, SUMR), (1, SQR)):
                    prog.wait("vector", "pe", pecnt[q])
                    ev = prog.op("vector",
                                 lambda E, o=dstrow[0:1, half * 512:half * 512 + 512],
                                 i=PS[q][0:1, 0:512]: E.tensor_copy(out=o, in_=i), inc="dv")
                    banks.release(q, "dv", ev)

            prog.op("vector", lambda E: E.tensor_scalar(out=MR, in0=SUMR, scalar1=1.0 / D,
                                                        scalar2=None, op0=ALU.mult))
            prog.op("vector", lambda E: E.tensor_scalar(out=SQR, in0=SQR, scalar1=1.0 / D,
                                                        scalar2=EPS, op0=ALU.mult, op1=ALU.add))
            prog.op("vector", lambda E: E.tensor_tensor(out=VARR, in0=MR, in1=MR, op=ALU.mult))
            dv_c = prog.op("vector", lambda E: E.tensor_tensor(out=VARR, in0=SQR, in1=VARR,
                                                               op=ALU.subtract), inc="dv")
            prog.wait("scalar", "dv", dv_c)
            ac_c = prog.op("scalar", lambda E: E.activation(out=STDR, in_=VARR, func=AF.Sqrt),
                           inc="ac")
            prog.wait("vector", "ac", ac_c)
            dv_c = prog.op("vector", lambda E: E.reciprocal(out=RSTDR, in_=STDR), inc="dv")
            prog.wait("tensor", "dv", dv_c)

            last = None
            for half in (0, 1):
                banks.acquire(2)
                banks.acquire(3)
                for bank, row in ((2, MR), (3, RSTDR)):
                    pe_c = prog.op("tensor",
                                   lambda E, o=psf(bank), r=row[0:1, half * 512:half * 512 + 512]:
                                   E.matmul(out=o, lhsT=S_ONESF[:], rhs=r, start=True, stop=True),
                                   inc="pe")
                prog.wait("vector", "pe", pe_c)
                for kc in range(KCN):
                    sl = slice(kc * T + half * 512, kc * T + half * 512 + 512)
                    prog.op("vector", lambda E, o=SCR, a=XT[:, sl], b=psf(2):
                            E.tensor_tensor(out=o, in0=a, in1=b, op=ALU.subtract))
                    last = prog.op("vector", lambda E, o=XBF[:, sl], a=SCR, b=psf(3):
                                   E.tensor_tensor(out=o, in0=a, in1=b, op=ALU.mult),
                                   inc=("dv" if kc == KCN - 1 else None))
                banks.release(2, "dv", last)
                banks.release(3, "dv", last)
            return xbf_o, last

        def emit_qkv(layer, xbf_o, xbf_ready):
            XBF = AR.bf(xbf_o, KCN * T)
            AR.reset(2 * KCN * T)
            qt_o = AR.alloc(2 * KCN * T)
            wblk_o = AR.alloc(2 * KCN * 256 * 2)
            vw_o = AR.alloc(2 * KCN * 512 * 2)
            kst_o = AR.alloc(2 * T * 2)
            vst_o = AR.alloc(2 * 512 * 2)
            QTv = AR.bf(qt_o, KCN * T)

            wq3 = wqkv[layer].rearrange("(kc p) n -> p kc n", p=128)
            ccsem = "cc%d" % layer
            prog.wait("tensor", "dv", xbf_ready)
            prog.wait("sync", "dv", xbf_ready)
            prog.wait("sync", "pe", prog.sc["pe"])
            state = {}

            def proj_fm(ob0, nobs, kind):
                for ip in range(nobs // 2):
                    buf = ip % 2
                    obA = ob0 + 2 * ip
                    wdst = AR.bf(wblk_o + buf * KCN * 256 * 2, KCN * 256)
                    key = ("wblkld", buf)
                    if ("wblk_prev",) in state:
                        prog.wait("sync", "pe", state[("wblk_prev",)])
                    lc = prog.dma("sync", "ld",
                                  wdst.rearrange("p (kc n) -> p kc n", kc=KCN),
                                  wq3[:, :, obA * 128:(obA + 2) * 128])
                    pe_c = None
                    for sub in range(2):
                        ob = obA + sub
                        prog.wait("tensor", "ld", lc)
                        for half in (0, 1):
                            bank = 4 + (2 * (2 * ip + sub) + half) % 2
                            banks.acquire(bank)
                            for kc in range(KCN):
                                lhs = wdst[:, kc * 256 + sub * 128: kc * 256 + sub * 128 + 128]
                                rhs = XBF[:, kc * T + half * 512: kc * T + half * 512 + 512]
                                pe_c = prog.op(
                                    "tensor",
                                    lambda E, o=psf(bank), l=lhs, r=rhs, st=(kc == 0),
                                    sp=(kc == KCN - 1):
                                    E.matmul(out=o, lhsT=l, rhs=r, start=st, stop=sp),
                                    inc=("pe" if kc == KCN - 1 else None))
                            prog.wait("vector", "pe", pe_c)
                            if kind == "K":
                                kdst = AR.bf(kst_o + buf * T * 2, T)[:, half * 512:half * 512 + 512]
                                if half == 0 and ("kst", buf) in state:
                                    prog.wait("vector", "gd", state[("kst", buf)])
                                ev = prog.op("vector", lambda E, o=kdst, i=psf(bank):
                                             E.tensor_copy(out=o, in_=i), inc="dv")
                            else:
                                qdst = QTv[:, ob * T + half * 512: ob * T + half * 512 + 512]
                                if layer == 1:
                                    ev = prog.op("vector",
                                                 lambda E, o=qdst, i=psf(bank), b=S_BQ[:, ob:ob + 1]:
                                                 E.tensor_scalar(out=o, in0=i, scalar1=b,
                                                                 scalar2=None, op0=ALU.add),
                                                 inc="dv")
                                else:
                                    ev = prog.op("vector", lambda E, o=qdst, i=psf(bank):
                                                 E.tensor_copy(out=o, in_=i), inc="dv")
                            banks.release(bank, "dv", ev)
                        if kind == "K":
                            prog.wait("gpsimd", "dv", ev)
                            g = prog.dma("gpsimd", "gd",
                                         cc_in[layer].ap()[(ob - 16) * 128:(ob - 15) * 128, :],
                                         AR.bf(kst_o + buf * T * 2, T))
                            state[("kst", buf)] = g
                    state[("wblkld", buf)] = pe_c
                    state[("wblk_prev",)] = pe_c

            proj_fm(16, 16, "K")

            for dg in range(4):
                buf = dg % 2
                wdst = AR.bf(vw_o + buf * KCN * 512 * 2, KCN * 512)
                key = ("vld", buf)
                if key in state:
                    prog.wait("sync", "pe", state[key])
                lc = prog.dma("sync", "ld",
                              wdst.rearrange("p (kc n) -> p kc n", kc=KCN),
                              wq3[:, :, 4096 + dg * 512: 4096 + (dg + 1) * 512])
                prog.wait("tensor", "ld", lc)
                pe_c = None
                for tt in range(8):
                    bank = 4 + tt % 2
                    banks.acquire(bank)
                    for kc in range(KCN):
                        lhs = XBF[:, kc * T + tt * 128: kc * T + tt * 128 + 128]
                        rhs = wdst[:, kc * 512:(kc + 1) * 512]
                        pe_c = prog.op("tensor",
                                       lambda E, o=psf(bank), l=lhs, r=rhs, st=(kc == 0),
                                       sp=(kc == KCN - 1):
                                       E.matmul(out=o, lhsT=l, rhs=r, start=st, stop=sp),
                                       inc=("pe" if kc == KCN - 1 else None))
                    sb = tt % 2
                    vdst = AR.bf(vst_o + sb * 512 * 2, 512)
                    prog.wait("vector", "pe", pe_c)
                    if ("vst", sb) in state:
                        prog.wait("vector", "gd", state[("vst", sb)])
                    ev = prog.op("vector", lambda E, o=vdst, i=psf(bank):
                                 E.tensor_copy(out=o, in_=i), inc="dv")
                    banks.release(bank, "dv", ev)
                    prog.wait("gpsimd", "dv", ev)
                    dst_ap = bass.AP(cc_in[layer],
                                     (D + 2 * tt * 128 + dg // 2) * T + (dg % 2) * 512,
                                     [[2 * T, 128], [1, 512]])
                    g = prog.dma("gpsimd", "gd", dst_ap, vdst)
                    state[("vst", sb)] = g
                state[("vld", buf)] = pe_c

            prog.wait("gpsimd", "gd", prog.sc["gd"])
            prog.op("gpsimd", lambda E, l=layer: E.collective_compute(
                "AllGather", ALU.bypass, replica_groups=[list(range(NCORES))],
                ins=[cc_in[l].ap().opt()], outs=[cc_out[l].ap().opt()]), inc=ccsem)

            proj_fm(0, 16, "Q")
            return qt_o, prog.sc["dv"]

        def emit_attention(layer, qt_o, q_ready):
            QTv = AR.bf(qt_o, KCN * T)
            AR.reset(4 * KCN * T)
            p_o = [AR.alloc(2 * 1024 * (k + 1)) for k in range(4)]
            s_o = AR.alloc(4 * 4096)
            kbuf_o = [AR.alloc(2 * 4096) for _ in range(2)]
            vbuf_o = AR.alloc(2 * 4096)
            pt_o = AR.alloc(4 * 512 * 2)
            stat_o = AR.alloc(4 * 64)
            slack_o = AR.alloc(4 * 512)
            attnT_o = 0
            wo_o = s_o
            SST = AR.f32(s_o, 4096)
            MXA = AR.f32(stat_o, 32)
            MCOLS = AR.f32(stat_o + 128, 4)
            NEGBS = AR.f32(stat_o + 144, 4)
            LCOLS = AR.f32(stat_o + 160, 4)
            RLS = AR.f32(stat_o + 176, 4)
            ATT = AR.bf(attnT_o, KCN * 512)
            SLACK = AR.f32(slack_o, 512)
            ccsem = "cc%d" % layer
            wo3 = wout[layer].rearrange("(kc p) n -> p kc n", p=128)

            prog.wait("scalar", ccsem, 1)
            prog.wait("tensor", "dv", q_ready)

            kvfree = {0: None, 1: None}
            kv_done = {}

            def issue_kv(s, h, slot):
                if kvfree[slot] is not None:
                    prog.wait("scalar", "pe", kvfree[slot])
                KB4 = AR.bf(kbuf_o[slot], 4096).rearrange("p (kb o jj) -> p kb o jj", o=8, jj=64)
                VB4 = AR.bf(vbuf_o, 4096).rearrange("p (kb q d) -> p kb q d", q=4, d=128)
                c = None
                for o in range(NCORES):
                    srck = bass.AP(cc_out[layer], (o * 2 * D + h * 128) * T + s * 512,
                                   [[T, 128], [64, 8], [1, 1], [1, 64]])
                    c = prog.dma("scalar", "kv", KB4[:, :, o:o + 1, :], srck)
                if s is not None and True:
                    pass
                return c

            def issue_v(s, h, after_pe):
                if after_pe is not None:
                    prog.wait("scalar", "pe", after_pe)
                VB4 = AR.bf(vbuf_o, 4096).rearrange("p (kb q d) -> p kb q d", q=4, d=128)
                c = None
                for o in range(NCORES):
                    srcv = bass.AP(cc_out[layer],
                                   (o * 2 * D + D + 2 * (s * 512) + h // 8) * T + (h % 8) * 128,
                                   [[2 * T, 64], [128 * T, 8], [1, 1], [1, 128]])
                    dstv = VB4[64 * (o % 2):64 * (o % 2) + 64, :, o // 2:o // 2 + 1, :]
                    c = prog.dma("scalar", "kv", dstv, srcv)
                return c

            order = [(s, h) for s in (0, 1) for h in range(H)]
            kslot = {}
            kv_k = {}
            kv_v = {}
            kslot[order[0]] = 0
            kv_k[order[0]] = issue_kv(*order[0], 0)
            kv_v[order[0]] = issue_v(*order[0], None)

            prev_pv = None
            prev_x3 = [None]
            head_done = {}
            prog.op("vector", lambda E: E.memset(NEGBS[:, 0:4], 0.0))
            wo_free = {}
            last_exp_of_chunk = {0: None, 1: None}

            for idx, (s, h) in enumerate(order):
                slot = kslot[(s, h)]
                if idx + 1 < len(order):
                    kslot[order[idx + 1]] = 1 - slot
                    kv_k[order[idx + 1]] = issue_kv(*order[idx + 1], 1 - slot)
                KB = AR.bf(kbuf_o[slot], 4096)
                VB = AR.bf(vbuf_o, 4096)
                prog.wait("tensor", "kv", kv_k[(s, h)])

                pn = {}
                x_c = {}
                dve_n = [0]

                def dop(fn, inc=None):
                    dve_n[0] += 1
                    return prog.op("vector", fn, inc=inc)

                def pad_to(target):
                    while dve_n[0] < target:
                        dop(lambda E: E.tensor_copy(out=SLACK[:, 0:64], in_=SST[:, 0:64]))

                pos = {}

                def stage_E(k):
                    nb = 2 * (k + 1)
                    for kb in range(nb):
                        bank = kb % 2
                        banks.acquire(bank)
                        lhs = QTv[:, h * T + s * 512 + k * 128: h * T + s * 512 + k * 128 + 128]
                        rhs = KB[:, kb * 512:(kb + 1) * 512]
                        pc = prog.op("tensor", lambda E, o=psf(bank), l=lhs, r=rhs:
                                     E.matmul(out=o, lhsT=l, rhs=r, start=True, stop=True),
                                     inc="pe")
                        prog.wait("vector", "pe", pc)
                        if kb == 0:
                            if k >= 1:
                                prog.wait("vector", "ac", x_c[k - 1])
                            elif prev_x3[0] is not None:
                                prog.wait("vector", "ac", prev_x3[0])
                        dst = SST[:, kb * 512:(kb + 1) * 512]
                        if kb < nb - 2:
                            ev = dop(lambda E, o=dst, i=psf(bank), m=MXA[:, 8 * k + kb:8 * k + kb + 1]:
                                     E.tensor_scalar(out=o, in0=i, scalar1=1.0, scalar2=None,
                                                     op0=ALU.mult, op1=ALU.max, accum_out=m),
                                     inc="dv")
                        else:
                            mi = kb - (nb - 2)
                            dop(lambda E, o=dst, i=psf(bank),
                                mk=S_MASK[:, mi * 512:(mi + 1) * 512]:
                                E.scalar_tensor_tensor(out=o, in0=i, scalar=1.0, in1=mk,
                                                       op0=ALU.mult, op1=ALU.add))
                            ev = dop(lambda E, o=MXA[:, 8 * k + kb:8 * k + kb + 1], i=dst:
                                     E.reduce_max(out=o, in_=i, axis=AX.X), inc="dv")
                        banks.release(bank, "dv", ev)

                def stage_M(k):
                    nb = 2 * (k + 1)
                    dop(lambda E, o=MCOLS[:, k:k + 1], i=MXA[:, 8 * k:8 * k + nb]:
                        E.reduce_max(out=o, in_=i, axis=AX.X))
                    pos[("M", k)] = dve_n[0]

                def stage_N(k):
                    pad_to(pos[("M", k)] + 6)
                    c = dop(lambda E, o=NEGBS[:, k:k + 1], i=MCOLS[:, k:k + 1]:
                            E.tensor_scalar(out=o, in0=i, scalar1=-CSC, scalar2=None,
                                            op0=ALU.mult), inc="dv")
                    pos[("N", k)] = c

                def stage_X(k):
                    nb = 2 * (k + 1)
                    Pk = AR.bf(p_o[k], 1024 * (k + 1))
                    prog.wait("scalar", "dv", pos[("N", k)])
                    if prev_pv is not None:
                        prog.wait("scalar", "pe", prev_pv)
                    x_c[k] = prog.op("scalar", lambda E, o=Pk, i=SST[:, 0:nb * 512],
                                     b=NEGBS[:, k:k + 1]:
                                     E.activation(out=o, in_=i, func=AF.Exp, scale=CSC,
                                                  bias=b), inc="ac")
                    last_exp_of_chunk[s] = x_c[k]
                    if k == 3:
                        prev_x3[0] = x_c[k]

                def stage_L(k):
                    Pk = AR.bf(p_o[k], 1024 * (k + 1))
                    prog.wait("vector", "ac", x_c[k])
                    dop(lambda E, o=SLACK[:, 0:64], i=SST[:, 0:64]: E.tensor_copy(out=o, in_=i))
                    dop(lambda E, o=LCOLS[:, k:k + 1], i=Pk:
                        E.reduce_sum(out=o, in_=i, axis=AX.X))
                    pos[("L", k)] = dve_n[0]

                def stage_R(k):
                    pad_to(pos[("L", k)] + 6)
                    dop(lambda E, o=RLS[:, k:k + 1], i=LCOLS[:, k:k + 1]:
                        E.reciprocal(out=o, in_=i))
                    pos[("R", k)] = dve_n[0]

                def stage_B(k):
                    pad_to(pos[("R", k)] + 8)
                    Pk = AR.bf(p_o[k], 1024 * (k + 1))
                    pn[k] = dop(lambda E, o=Pk, i=Pk, r=RLS[:, k:k + 1]:
                                E.tensor_scalar(out=o, in0=i, scalar1=r, scalar2=None,
                                                op0=ALU.mult), inc="dv")

                for g in range(7):
                    if g <= 3:
                        stage_E(g)
                        stage_M(g)
                    if 0 <= g - 1 <= 3:
                        stage_L(g - 1)
                    if 0 <= g - 2 <= 3:
                        stage_R(g - 2)
                    if 0 <= g - 3 <= 3:
                        stage_B(g - 3)
                    if g <= 3:
                        stage_N(g)
                        stage_X(g)

                attn_bank = 6 + idx % 2
                banks.acquire(attn_bank)
                for k in range(4):
                    prog.wait("tensor", "dv", pn[k])
                prog.wait("tensor", "kv", kv_v[(s, h)])
                pv_c = None
                ptev = {}
                NPT = 4
                def emit_tr(kt):
                    start_q = kt // 8
                    nq = 4 - start_q
                    tb = 2 + kt % 2
                    banks.acquire(tb)
                    tr_c = None
                    for qi in range(start_q, 4):
                        Pq = AR.bf(p_o[qi], 1024 * (qi + 1))
                        tr_c = prog.op(
                            "tensor",
                            lambda E, o=psbf(tb)[:, (qi - start_q) * 128:(qi - start_q) * 128 + 128],
                            i=Pq[:, kt * 128:kt * 128 + 128]:
                            E.transpose(out=o, in_=i, identity=S_ID[:]),
                            inc=("pe" if qi == 3 else None))
                    ptbuf = AR.bf(pt_o + (kt % NPT) * 1024, nq * 128)
                    prog.wait("scalar", "pe", tr_c)
                    if kt >= NPT:
                        prog.wait("scalar", "pe", pvs[kt - NPT])
                    pv = prog.op("scalar", lambda E, o=ptbuf, i=psbf(tb)[:, 0:nq * 128]:
                                 E.activation(out=o, in_=i, func=AF.Copy), inc="ac")
                    banks.release(tb, "ac", pv)
                    ptev[kt] = (pv, ptbuf)
                pvs = {}
                emit_tr(0)
                for kt in range(32):
                    if kt + 1 < 32:
                        emit_tr(kt + 1)
                    start_q = kt // 8
                    pev, ptbuf = ptev[kt]
                    prog.wait("tensor", "ac", pev)
                    pv_c = prog.op("tensor",
                                   lambda E, o=psf(attn_bank)[:, start_q * 128:512],
                                   l=VB[:, kt * 128:kt * 128 + 128], r=ptbuf,
                                   st=(kt == 0), sp=(kt == 31):
                                   E.matmul(out=o, lhsT=l, rhs=r, start=st, stop=sp),
                                   inc=("pe" if kt == 31 else None))
                    pvs[kt] = prog.sc["pe"] if pv_c is None else pv_c
                prev_pv = pv_c
                kvfree[slot] = pv_c
                prog.wait("vector", "pe", pv_c)
                if layer == 1:
                    aev = prog.op("vector",
                                  lambda E, o=ATT[:, h * 512:(h + 1) * 512], i=psf(attn_bank),
                                  b=S_BV[:, h:h + 1]:
                                  E.tensor_scalar(out=o, in0=i, scalar1=b, scalar2=None,
                                                  op0=ALU.add), inc="dv")
                else:
                    aev = prog.op("vector",
                                  lambda E, o=ATT[:, h * 512:(h + 1) * 512], i=psf(attn_bank):
                                  E.tensor_copy(out=o, in_=i), inc="dv")
                banks.release(attn_bank, "dv", aev)
                head_done[(s, h)] = aev
                if DBG_ATTN and layer == 0 and s == 0 and h == 0:
                    prog.wait("gpsimd", "dv", aev)
                    prog.dma("gpsimd", "gd", dbg_p[:, 0:1024], AR.bf(p_o[0], 1024))
                    prog.dma("gpsimd", "gd", dbg_p[:, 1024:3072], AR.bf(p_o[1], 2048))
                    prog.dma("gpsimd", "gd", dbg_p[:, 3072:6144], AR.bf(p_o[2], 3072))
                    prog.dma("gpsimd", "gd", dbg_p[:, 6144:10240], AR.bf(p_o[3], 4096))
                    prog.dma("gpsimd", "gd", dbg_sst[:, :], SST)
                    prog.dma("gpsimd", "gd", dbg_st2[:, :], AR.f32(stat_o, 16))
                    prog.dma("gpsimd", "gd", dbg_kb[:, :], AR.bf(kbuf_o[slot], 4096))
                    prog.dma("gpsimd", "gd", dbg_vb[:, :], AR.bf(vbuf_o, 4096))
                    dbgc = prog.dma("gpsimd", "gd", dbg_ath[:, :], ATT[:, 0:512])
                    prog.wait("vector", "gd", dbgc)
                    prog.wait("tensor", "gd", dbgc)
                    prog.wait("scalar", "gd", dbgc)
                if idx + 1 < len(order):
                    prog.wait("scalar", "dv", aev)
                    kv_v[order[idx + 1]] = issue_v(*order[idx + 1], None)

                if h == H - 1:
                    wo_pe = {}
                    for oc in range(KCN):
                        buf = oc % 2
                        wdst = AR.bf(wo_o + buf * KCN * 128 * 2, KCN * 128)
                        if oc < 2:
                            prog.wait("sync", "ac", last_exp_of_chunk[s])
                        if oc >= 1 and ("wores",) in wo_pe:
                            prog.wait("sync", "dv", wo_pe[("wores",)])
                        lc = prog.dma("sync", "ld",
                                      wdst.rearrange("p (kc n) -> p kc n", kc=KCN),
                                      wo3[:, :, oc * 128:(oc + 1) * 128])
                        bank = 2 + oc % 2
                        banks.acquire(bank)
                        prog.wait("tensor", "ld", lc)
                        if oc == 0:
                            for hh in range(H):
                                prog.wait("tensor", "dv", head_done[(s, hh)])
                        pc = None
                        for kc in range(KCN):
                            pc = prog.op("tensor",
                                         lambda E, o=psf(bank), l=wdst[:, kc * 128:(kc + 1) * 128],
                                         r=ATT[:, kc * 512:(kc + 1) * 512],
                                         st=(kc == 0), sp=(kc == KCN - 1):
                                         E.matmul(out=o, lhsT=l, rhs=r, start=st, stop=sp),
                                         inc=("pe" if kc == KCN - 1 else None))
                        wo_pe[oc] = pc
                        prog.wait("vector", "pe", pc)
                        xsl = XT[:, oc * T + s * 512: oc * T + s * 512 + 512]
                        bias = S_BP[:, oc:oc + 1] if layer == 1 else 0.0
                        rc = prog.op("vector",
                                     lambda E, o=xsl, i=psf(bank), b=bias, x=xsl:
                                     E.scalar_tensor_tensor(out=o, in0=i, scalar=b, in1=x,
                                                            op0=ALU.add, op1=ALU.add), inc="dv")
                        banks.release(bank, "dv", rc)
                        wo_pe[("wores",)] = rc
            return prog.sc["dv"]

        def emit_ffn0(xbf_o):
            XBF = AR.bf(xbf_o, KCN * T)
            AR.reset(2 * KCN * T)
            h_o = AR.alloc(2 * IC0 * 512)
            gw_o = AR.alloc(2 * KCN * 256 * 2)
            uw_o = AR.alloc(2 * KCN * 256 * 2)
            dw_o = AR.alloc(2 * 22 * 128 * 2)
            ga_o = AR.alloc(4 * 512 * 2)

            prog.wait("sync", "dv", prog.sc["dv"])
            prog.wait("sync", "pe", prog.sc["pe"])
            gw3 = gwTd.rearrange("(kc p) n -> p kc n", p=128)
            uw3 = uwTd.rearrange("(kc p) n -> p kc n", p=128)
            dw3 = dwTd.rearrange("(ic p) n -> p ic n", p=128)

            state = {}

            def _emit_h(state, key):
                sc_, gact, ubank, icx = state.pop(key)
                prog.wait("vector", "ac", sc_)
                hdst = AR.bf(h_o + icx * 1024, 512)
                hv = prog.op("vector",
                             lambda E, o=hdst, i=psf(ubank), s=S_US[:, icx:icx + 1], g=gact:
                             E.scalar_tensor_tensor(out=o, in0=i, scalar=s, in1=g,
                                                    op0=ALU.mult, op1=ALU.mult), inc="dv")
                banks.release(ubank, "dv", hv)
                state[("h", icx)] = hv

            for tt in range(2):
                if tt == 1 and ("downlast",) in state:
                    pass
                for ipr in range(IC0 // 2):
                    buf = ipr % 2
                    gwb = AR.bf(gw_o + buf * KCN * 256 * 2, KCN * 256)
                    uwb = AR.bf(uw_o + buf * KCN * 256 * 2, KCN * 256)
                    if ("gld_prev",) in state:
                        prog.wait("sync", "pe", state[("gld_prev",)])
                    prog.dma("sync", "ld", gwb.rearrange("p (kc n) -> p kc n", kc=KCN),
                             gw3[:, :, ipr * 256:(ipr + 1) * 256])
                    lu = prog.dma("sync", "ld", uwb.rearrange("p (kc n) -> p kc n", kc=KCN),
                                  uw3[:, :, ipr * 256:(ipr + 1) * 256])
                    pe_last = None
                    for sub in range(2):
                        ic = 2 * ipr + sub
                        gbank = (ic % 2) * 2
                        ubank = gbank + 1
                        prog.wait("tensor", "ld", lu)
                        if tt == 1 and ipr == 0 and sub == 0:
                            prog.wait("vector", "pe", state[("downlast",)])
                        banks.acquire(gbank)
                        banks.acquire(ubank)
                        pc = None
                        for kc in range(KCN):
                            rhs = XBF[:, kc * T + tt * 512: kc * T + tt * 512 + 512]
                            prog.op("tensor",
                                    lambda E, o=psf(gbank),
                                    l=gwb[:, kc * 256 + sub * 128: kc * 256 + sub * 128 + 128],
                                    r=rhs, st=(kc == 0), sp=(kc == KCN - 1):
                                    E.matmul(out=o, lhsT=l, rhs=r, start=st, stop=sp))
                            pc = prog.op("tensor",
                                         lambda E, o=psf(ubank),
                                         l=uwb[:, kc * 256 + sub * 128: kc * 256 + sub * 128 + 128],
                                         r=rhs, st=(kc == 0), sp=(kc == KCN - 1):
                                         E.matmul(out=o, lhsT=l, rhs=r, start=st, stop=sp),
                                         inc=("pe" if kc == KCN - 1 else None))
                        pe_last = pc
                        prog.wait("scalar", "pe", pc)
                        gact = AR.f32(ga_o + (ic % 2) * 2048, 512)
                        sc_ = prog.op("scalar",
                                      lambda E, o=gact, i=psf(gbank), s=S_GS[:, ic:ic + 1]:
                                      E.activation(out=o, in_=i, func=AF.Silu, scale=s), inc="ac")
                        banks.release(gbank, "ac", sc_)
                        state[("silu", ic)] = (sc_, gact, ubank, ic)
                        if ic >= 1:
                            _emit_h(state, ("silu", ic - 1))
                    state[("gld_prev",)] = pe_last

                _emit_h(state, ("silu", IC0 - 1))
                for hc in range(KCN):
                    ldh = {}
                    for hf in range(2):
                        if ("dres",) in state:
                            prog.wait("sync", "dv", state[("dres",)])
                        ldh[hf] = prog.dma(
                            "sync", "ld",
                            AR.bf(dw_o + hf * 22 * 128 * 2, 22 * 128)
                            .rearrange("p (ic n) -> p ic n", ic=22),
                            dw3[:, hf * 22:(hf + 1) * 22, hc * 128:(hc + 1) * 128])
                    bank = 4 + hc % 2
                    banks.acquire(bank)
                    if hc == 0:
                        for ic in range(IC0):
                            prog.wait("tensor", "dv", state[("h", ic)])
                    pc = None
                    for hf in range(2):
                        prog.wait("tensor", "ld", ldh[hf])
                        for ici in range(22):
                            ic = hf * 22 + ici
                            pc = prog.op("tensor",
                                         lambda E, o=psf(bank),
                                         l=AR.bf(dw_o + hf * 22 * 128 * 2 + ici * 256, 128),
                                         r=AR.bf(h_o + ic * 1024, 512),
                                         st=(ic == 0), sp=(ic == IC0 - 1):
                                         E.matmul(out=o, lhsT=l, rhs=r, start=st, stop=sp),
                                         inc=("pe" if ic == IC0 - 1 else None))
                    prog.wait("vector", "pe", pc)
                    xsl = XT[:, hc * T + tt * 512: hc * T + tt * 512 + 512]
                    rc = prog.op("vector",
                                 lambda E, o=xsl, i=psf(bank), s=S_DS[:, hc:hc + 1], x=xsl:
                                 E.scalar_tensor_tensor(out=o, in0=i, scalar=s, in1=x,
                                                        op0=ALU.mult, op1=ALU.add), inc="dv")
                    banks.release(bank, "dv", rc)
                    state[("downlast",)] = pc
                    state[("dres",)] = rc
            return prog.sc["dv"]

        def emit_moe(xbf_o):
            XBF = AR.bf(xbf_o, KCN * T)
            AR.reset(2 * KCN * T)
            tmp_o = AR.alloc(4 * KCN * 128)
            lgw_o = AR.alloc(4 * 256)
            sel_o = AR.alloc(4 * 1024)
            h_o = AR.alloc(2 * IC1 * T)
            gw_o = AR.alloc(2 * KCN * 256 * 2)
            uw_o = AR.alloc(2 * KCN * 256 * 2)
            dwm_o = AR.alloc(2 * IC1 * 128 * 2)
            wb_o = AR.alloc(4 * T)
            rowb_o = AR.alloc(4 * T)
            t1_o = AR.alloc(4 * 512 * 2)

            prog.wait("sync", "dv", prog.sc["dv"])
            prog.wait("sync", "pe", prog.sc["pe"])
            selT = AR.f32(sel_o, 1024, rows=8)
            WB = AR.f32(wb_o, T)
            ROWB = AR.f32(rowb_o, T, rows=1)

            prog.wait("gpsimd", "dv", prog.sc["dv"])
            rq0 = prog.dma("gpsimd", "gd", rstd_scr.ap()[0:1, :], RSTDR)
            prog.wait("gpsimd", "gd", rq0)
            rq = prog.op("gpsimd", lambda E, d=RSTDC,
                         s=rstd_scr.ap().rearrange("o (t p) -> (o p) t", p=128):
                         E.dma_start(out=d, in_=s, allow_slow_non_contiguous=True),
                         inc="gd", amt=16)

            ROUT = AR.f32(lgw_o, 256)
            def rt(t, a, b):
                return AR.f32(lgw_o + 4 * (32 * t + a), b - a)

            sel_done = None
            for ttile in range(8):
                banks.acquire(2)
                pe_b = prog.op("tensor",
                               lambda E, o=psf(2)[:, 0:128],
                               r=MR[0:1, ttile * 128:(ttile + 1) * 128]:
                               E.matmul(out=o, lhsT=S_ONESF[:], rhs=r, start=True, stop=True),
                               inc="pe")
                prog.wait("vector", "pe", pe_b)
                tc = None
                for kc in range(KCN):
                    tc = prog.op("vector",
                                 lambda E, o=AR.f32(tmp_o + kc * 512, 128),
                                 a=XT[:, kc * T + ttile * 128: kc * T + ttile * 128 + 128],
                                 b=psf(2)[:, 0:128]:
                                 E.tensor_tensor(out=o, in0=a, in1=b, op=ALU.subtract),
                                 inc=("dv" if kc == KCN - 1 else None))
                banks.release(2, "dv", tc)
                prog.wait("tensor", "dv", tc)
                banks.acquire(3)
                pc = None
                for kc in range(KCN):
                    pc = prog.op("tensor",
                                 lambda E, o=PS[3][:, 0:8], l=AR.f32(tmp_o + kc * 512, 128),
                                 r=S_RWT[:, kc * 8:(kc + 1) * 8],
                                 st=(kc == 0), sp=(kc == KCN - 1):
                                 E.matmul(out=o, lhsT=l, rhs=r, start=st, stop=sp),
                                 inc=("pe" if kc == KCN - 1 else None))
                prog.wait("vector", "pe", pc)
                ev = prog.op("vector", lambda E, o=rt(ttile, 0, 8), i=PS[3][:, 0:8]:
                             E.tensor_copy(out=o, in_=i), inc="dv")
                banks.release(3, "dv", ev)

            for t in range(8):
                prog.op("vector", lambda E, o=rt(t, 24, 25), i=rt(t, 0, 8):
                        E.reduce_max(out=o, in_=i, axis=AX.X))
            for t in range(8):
                prog.op("vector", lambda E, o=rt(t, 8, 16), i=rt(t, 0, 8), v=rt(t, 24, 25):
                        E.tensor_scalar(out=o, in0=i, scalar1=v, scalar2=None, op0=ALU.is_ge))
            for t in range(8):
                prog.op("vector", lambda E, o=rt(t, 16, 24), i=rt(t, 8, 16), z=S_IOTA[:]:
                        E.scalar_tensor_tensor(out=o, in0=i, scalar=16.0, in1=z,
                                               op0=ALU.mult, op1=ALU.subtract))
            for t in range(8):
                prog.op("vector", lambda E, o=rt(t, 25, 26), i=rt(t, 16, 24):
                        E.reduce_max(out=o, in_=i, axis=AX.X))
            for t in range(8):
                prog.op("vector", lambda E, o=rt(t, 8, 16), i=rt(t, 16, 24), v=rt(t, 25, 26):
                        E.tensor_scalar(out=o, in0=i, scalar1=v, scalar2=None, op0=ALU.is_ge))
            for t in range(8):
                prog.op("vector", lambda E, o=rt(t, 16, 24), i=rt(t, 8, 16), l=rt(t, 0, 8):
                        E.scalar_tensor_tensor(out=o, in0=i, scalar=NEG, in1=l,
                                               op0=ALU.mult, op1=ALU.add))
            for t in range(8):
                prog.op("vector", lambda E, o=rt(t, 26, 27), i=rt(t, 16, 24):
                        E.reduce_max(out=o, in_=i, axis=AX.X))
            for t in range(8):
                prog.op("vector", lambda E, o=rt(t, 27, 28), i=rt(t, 24, 25), j=rt(t, 26, 27):
                        E.tensor_tensor(out=o, in0=i, in1=j, op=ALU.subtract))
            prog.wait("vector", "gd", rq)
            gp = None
            for t in range(8):
                gp = prog.op("vector", lambda E, o=rt(t, 27, 28), i=rt(t, 27, 28),
                             r=RSTDC[:, t:t + 1]:
                             E.tensor_tensor(out=o, in0=i, in1=r, op=ALU.mult),
                             inc=("dv" if t == 7 else None))
            prog.wait("scalar", "dv", gp)
            sg = None
            for t in range(8):
                sg = prog.op("scalar", lambda E, o=rt(t, 28, 29), i=rt(t, 27, 28):
                             E.activation(out=o, in_=i, func=AF.Sigmoid), inc="ac")
            for t in (0, 1):
                sg = prog.op("scalar", lambda E, o=rt(t, 29, 30), i=rt(t, 28, 29):
                             E.activation(out=o, in_=i, func=AF.Sigmoid), inc="ac")
            prog.wait("vector", "ac", sg)
            se = {}
            for t in range(8):
                se[t] = prog.op("vector", lambda E, o=rt(t, 8, 16), i=rt(t, 8, 16),
                                w=rt(t, 28, 29):
                                E.tensor_scalar(out=o, in0=i, scalar1=w, scalar2=None,
                                                op0=ALU.mult), inc="dv")
            for ttile in range(8):
                prog.wait("tensor", "dv", se[ttile])
                banks.acquire(4)
                tr = prog.op("tensor", lambda E, o=PS[4][0:8, 0:128], i=rt(ttile, 8, 16):
                             E.transpose(out=o, in_=i, identity=S_IDF[:]), inc="pe")
                prog.wait("vector", "pe", tr)
                sel_done = prog.op("vector",
                                   lambda E, o=selT[0:8, ttile * 128:(ttile + 1) * 128],
                                   i=PS[4][0:8, 0:128]:
                                   E.tensor_copy(out=o, in_=i), inc="dv")
                banks.release(4, "dv", sel_done)

            state = {}
            for e in range(E1):
                prog.wait("gpsimd", "dv", sel_done)
                if ("rowb",) in state:
                    prog.wait("gpsimd", "pe", state[("rowb",)])
                rb = prog.dma("gpsimd", "gd", ROWB, selT[e:e + 1, :])
                prog.wait("tensor", "gd", rb)
                wb_pe = {}
                for half in (0, 1):
                    bank = 6 + half
                    banks.acquire(bank)
                    wb_pe[half] = prog.op(
                        "tensor",
                        lambda E, o=psf(bank), r=ROWB[0:1, half * 512:half * 512 + 512]:
                        E.matmul(out=o, lhsT=S_ONESF[:], rhs=r, start=True, stop=True), inc="pe")
                state[("rowb",)] = wb_pe[1]
                for half in (0, 1):
                    prog.wait("vector", "pe", wb_pe[half])
                    wv = prog.op("vector",
                                 lambda E, o=AR.f32(wb_o + half * 2048, 512), i=psf(6 + half):
                                 E.tensor_copy(out=o, in_=i), inc="dv")
                    banks.release(6 + half, "dv", wv)
                wb_dv = wv
                if DBG_ATTN:
                    prog.wait("gpsimd", "dv", wb_dv)
                    prog.dma("gpsimd", "gd", dbg_wb[e:e + 1, :], AR.f32(wb_o, T, rows=1))
                    prog.dma("gpsimd", "gd", dbg_wb[8 + e:9 + e, :],
                             AR.f32(wb_o, T, rows=1, row0=64))
                    if e == 0:
                        prog.dma("gpsimd", "gd", dbg_selt[:, :], selT)

                groups = [(2 * i, 2) for i in range(5)] + [(10, 1)]
                for gi, (ic0g, gn) in enumerate(groups):
                    buf = gi % 2
                    gwb = AR.bf(gw_o + buf * KCN * 256 * 2, KCN * 256)
                    uwb = AR.bf(uw_o + buf * KCN * 256 * 2, KCN * 256)
                    if ("gld_prev",) in state:
                        prog.wait("sync", "pe", state[("gld_prev",)])
                    srcg = bass.AP(mgw, e * D * INTER1 + ic0g * 128,
                                   [[INTER1, 128], [128 * INTER1, KCN], [1, gn * 128]])
                    srcu = bass.AP(muw, e * D * INTER1 + ic0g * 128,
                                   [[INTER1, 128], [128 * INTER1, KCN], [1, gn * 128]])
                    prog.dma("sync", "ld",
                             gwb.rearrange("p (kc n) -> p kc n", kc=KCN)[:, :, 0:gn * 128], srcg)
                    lu = prog.dma("sync", "ld",
                                  uwb.rearrange("p (kc n) -> p kc n", kc=KCN)[:, :, 0:gn * 128],
                                  srcu)
                    pe_last = None
                    for sub in range(gn):
                        ic = ic0g + sub
                        for half in (0, 1):
                            gbank = (2 * ic + half) % 2 * 2
                            ubank = gbank + 1
                            prog.wait("tensor", "ld", lu)
                            if e == 0 and ("moehwar",) not in state:
                                state[("moehwar",)] = 1
                            banks.acquire(gbank)
                            banks.acquire(ubank)
                            pc = None
                            for kc in range(KCN):
                                rhs = XBF[:, kc * T + half * 512: kc * T + half * 512 + 512]
                                prog.op("tensor",
                                        lambda E, o=psf(gbank),
                                        l=gwb[:, kc * 256 + sub * 128: kc * 256 + sub * 128 + 128],
                                        r=rhs, st=(kc == 0), sp=(kc == KCN - 1):
                                        E.matmul(out=o, lhsT=l, rhs=r, start=st, stop=sp))
                                pc = prog.op("tensor",
                                             lambda E, o=psf(ubank),
                                             l=uwb[:, kc * 256 + sub * 128: kc * 256 + sub * 128 + 128],
                                             r=rhs, st=(kc == 0), sp=(kc == KCN - 1):
                                             E.matmul(out=o, lhsT=l, rhs=r, start=st, stop=sp),
                                             inc=("pe" if kc == KCN - 1 else None))
                            pe_last = pc
                            prog.wait("scalar", "pe", pc)
                            gact = AR.f32(t1_o + (ic % 2) * 2048, 512)
                            sc_ = prog.op("scalar", lambda E, o=gact, i=psf(gbank):
                                          E.activation(out=o, in_=i, func=AF.Silu), inc="ac")
                            banks.release(gbank, "ac", sc_)
                            prog.wait("vector", "ac", sc_)
                            if half == 0 and sub == 0 and gi == 0:
                                prog.wait("vector", "dv", wb_dv)
                                if ("hwar", e) in state:
                                    prog.wait("vector", "pe", state[("hwar", e)])
                            prog.op("vector",
                                    lambda E, o=gact, i=gact,
                                    w=AR.f32(wb_o + half * 2048, 512):
                                    E.tensor_tensor(out=o, in0=i, in1=w, op=ALU.mult))
                            hdst = AR.bf(h_o + ic * 2048 + half * 1024, 512)
                            hv = prog.op("vector",
                                         lambda E, o=hdst, i=psf(ubank), g=gact:
                                         E.scalar_tensor_tensor(out=o, in0=i, scalar=1.0, in1=g,
                                                                op0=ALU.mult, op1=ALU.mult),
                                         inc="dv")
                            banks.release(ubank, "dv", hv)
                            state[("h", ic)] = hv
                    state[("gld_prev",)] = pe_last

                for hc in range(KCN):
                    buf = hc % 2
                    if ("dres",) in state:
                        prog.wait("sync", "dv", state[("dres",)])
                    srcd = bass.AP(mdw, e * INTER1 * D + hc * 128,
                                   [[D, 128], [128 * D, IC1], [1, 128]])
                    lc = prog.dma("sync", "ld",
                                  AR.bf(dwm_o + buf * IC1 * 128 * 2, IC1 * 128)
                                  .rearrange("p (ic n) -> p ic n", ic=IC1), srcd)
                    prog.wait("tensor", "ld", lc)
                    if hc == 0:
                        for ic in range(IC1):
                            prog.wait("tensor", "dv", state[("h", ic)])
                    for half in (0, 1):
                        bank = 4 + (2 * hc + half) % 2
                        banks.acquire(bank)
                        pc = None
                        for ic in range(IC1):
                            pc = prog.op("tensor",
                                         lambda E, o=psf(bank),
                                         l=AR.bf(dwm_o + buf * IC1 * 128 * 2 + ic * 256, 128),
                                         r=AR.bf(h_o + ic * 2048 + half * 1024, 512),
                                         st=(ic == 0), sp=(ic == IC1 - 1):
                                         E.matmul(out=o, lhsT=l, rhs=r, start=st, stop=sp),
                                         inc=("pe" if ic == IC1 - 1 else None))
                        prog.wait("vector", "pe", pc)
                        xsl = XT[:, hc * T + half * 512: hc * T + half * 512 + 512]
                        rc = prog.op("vector",
                                     lambda E, o=xsl, i=psf(bank), x=xsl:
                                     E.scalar_tensor_tensor(out=o, in0=i, scalar=0.0, in1=x,
                                                            op0=ALU.add, op1=ALU.add), inc="dv")
                        banks.release(bank, "dv", rc)
                        state[("dres",)] = rc
                    state[("dld", buf)] = pc
                state[("hwar", e + 1)] = pc
            return prog.sc["dv"]

        r = 0
        xbf_o, r = emit_ln(r)
        qt_o = None
        if upto not in ("ln1_0",):
            qt_o, qr = emit_qkv(0, xbf_o, r)
        if upto not in ("ln1_0", "qkv_0"):
            r = emit_attention(0, qt_o, qr)
            if upto != "attn_0":
                xbf_o, r = emit_ln(r)
                r = emit_ffn0(xbf_o)
                if upto != "ffn_0":
                    xbf_o, r = emit_ln(r)
                    qt_o, qr = emit_qkv(1, xbf_o, r)
                    r = emit_attention(1, qt_o, qr)
                    if upto != "attn_1":
                        xbf_o, r = emit_ln(r)
                        r = emit_moe(xbf_o)

        prog.wait("gpsimd", "dv", prog.sc["dv"])
        prog.wait("gpsimd", "ac", prog.sc["ac"])
        prog.wait("gpsimd", "pe", prog.sc["pe"])
        if upto == "ln1_0":
            prog.dma("gpsimd", "gd",
                     dbg_xbf.rearrange("(kc p) t -> p kc t", p=128),
                     AR.bf(0, KCN * T).rearrange("p (kc t) -> p kc t", kc=KCN))
            prog.dma("gpsimd", "gd", dbg_row[0:1, :], MR)
            prog.dma("gpsimd", "gd", dbg_row[1:2, :], RSTDR)
        if upto == "qkv_0":
            prog.wait("gpsimd", "cc0", 1)
            prog.dma("gpsimd", "gd",
                     dbg_qt.rearrange("(kc p) t -> p kc t", p=128),
                     AR.bf(qt_o, KCN * T).rearrange("p (kc t) -> p kc t", kc=KCN))
            prog.dma("gpsimd", "gd", dbg_ccin[:, :], cc_in[0].ap()[:, :])
            prog.dma("gpsimd", "gd", dbg_ccout[:, :], cc_out[0].ap()[0:4 * D, :])
        st_c = prog.dma("gpsimd", "gd",
                        xout.rearrange("(kc p) t -> p kc t", p=128),
                        XT[:].rearrange("p (kc t) -> p kc t", kc=KCN))
        prog.wait("gpsimd", "gd", st_c)

        with nc.Block() as block:
            @block.sync
            def _(E):
                for f in prog.ops["sync"]:
                    f(E)

            @block.vector
            def _(E):
                for f in prog.ops["vector"]:
                    f(E)

            @block.scalar
            def _(E):
                for f in prog.ops["scalar"]:
                    f(E)

            @block.tensor
            def _(E):
                for f in prog.ops["tensor"]:
                    f(E)

            @block.gpsimd
            def _(E):
                for f in prog.ops["gpsimd"]:
                    f(E)

    return nc


def _host_inputs(inputs):
    x = np.asarray(inputs["L0_attn_x"], dtype=np.float32)

    def cast(a):
        return np.ascontiguousarray(np.asarray(a, dtype=np.float32)).astype(BF)

    shared = {
        "wqkv0": cast(inputs["L0_attn_w_qkv"]),
        "wout0": cast(inputs["L0_attn_w_out"]),
        "gwT0": cast(np.asarray(inputs["L0_ffn_gate_w"], np.float32).T),
        "uwT0": cast(np.asarray(inputs["L0_ffn_up_w"], np.float32).T),
        "dwT0": cast(np.asarray(inputs["L0_ffn_down_w"], np.float32).T),
        "wqkv1": cast(inputs["L1_attn_w_qkv"]),
        "wout1": cast(inputs["L1_attn_w_proj"]),
        "mgw": cast(np.asarray(inputs["L1_ffn_gate_w"], np.float32).transpose(0, 2, 1)),
        "muw": cast(np.asarray(inputs["L1_ffn_up_w"], np.float32).transpose(0, 2, 1)),
        "mdw": cast(np.asarray(inputs["L1_ffn_down_w"], np.float32).transpose(0, 2, 1)),
        "c_id": np.eye(128, dtype=BF),
        "c_idf": np.eye(128, dtype=np.float32),
        "c_iota": np.broadcast_to(np.arange(8, dtype=np.float32), (128, 8)).copy(),
        "c_onesbf": np.ones((128, 1), dtype=BF),
        "c_onesf": np.ones((1, 128), dtype=np.float32),
        "c_gs": np.asarray(inputs["L0_ffn_gate_s"], np.float32).reshape(IC0, 128).T.copy(),
        "c_us": np.asarray(inputs["L0_ffn_up_s"], np.float32).reshape(IC0, 128).T.copy(),
        "c_ds": np.asarray(inputs["L0_ffn_down_s"], np.float32).reshape(KCN, 128).T.copy(),
        "c_bq": np.asarray(inputs["L1_attn_b_qkv"], np.float32)[0:2048].reshape(KCN, 128).T.copy(),
        "c_bv": np.asarray(inputs["L1_attn_b_qkv"], np.float32)[4096:6144].reshape(H, 128).T.copy(),
        "c_bp": np.asarray(inputs["L1_attn_b_proj"], np.float32).reshape(KCN, 128).T.copy(),
        "c_rwt": np.ascontiguousarray(
            np.asarray(inputs["L1_ffn_router_w"], np.float32).T.reshape(KCN, 128, E1)
            .transpose(1, 0, 2).reshape(128, KCN * E1)),
    }

    in_maps = []
    i = np.arange(128)[:, None]
    col = np.arange(512)[None, :]
    r_ = col // 64
    jj = col % 64
    for c in range(NCORES):
        xc = np.empty((D, T), dtype=np.float32)
        for s in range(2):
            xc[:, s * 512:(s + 1) * 512] = x[s][c::8, :].T
        mask = np.empty((128, 1024), dtype=np.float32)
        for kbrel in range(2):
            valid = (c + 8 * i) >= (r_ + 512 * kbrel + 8 * jj)
            mask[:, kbrel * 512:(kbrel + 1) * 512] = np.where(valid, 0.0, NEG)
        m = dict(shared)
        m["x"] = np.ascontiguousarray(xc)
        m["c_mask"] = mask.astype(BF)
        in_maps.append(m)
    return in_maps


_NC_CACHE = {}


def kernel(**inputs):
    in_maps = _host_inputs(inputs)
    if "all" not in _NC_CACHE:
        _NC_CACHE["all"] = build_program(upto="all")
    nc = _NC_CACHE["all"]
    res = run_bass_kernel_spmd(nc, in_maps, core_ids=list(range(NCORES)))
    x = np.asarray(inputs["L0_attn_x"], dtype=np.float32)
    out = np.empty_like(x)
    for c in range(NCORES):
        xc = res.results[c]["xout"]
        for s in range(2):
            out[s][c::8, :] = xc[:, s * 512:(s + 1) * 512].T
    return out


# version 02
# speedup vs baseline: 1.0187x; 1.0187x over previous
import contextlib
import numpy as np
import ml_dtypes
import concourse.bass as bass
import concourse.mybir as mybir
from concourse.bass_utils import run_bass_kernel_spmd

F32 = mybir.dt.float32
BF16 = mybir.dt.bfloat16
AF = mybir.ActivationFunctionType
ALU = mybir.AluOpType
AX = mybir.AxisListType

NCORES = 8
T = 1024
D = 2048
KCN = 16
H = 16
NEG = -1.0e30
EPS = 1e-5
CSC = float(1.0 / np.sqrt(128.0))
INTER0 = 5632
IC0 = 44
E1 = 8
INTER1 = 1408
IC1 = 11

ENGINES = ("sync", "vector", "scalar", "tensor", "gpsimd")
BF = ml_dtypes.bfloat16


class Prog:
    def __init__(self):
        self.ops = {e: [] for e in ENGINES}
        self.sc = {}
        self.sems = {}

    def sem(self, name):
        self.sc[name] = 0

    def emit(self, eng, fn):
        self.ops[eng].append(fn)

    def wait(self, eng, sem, cnt):
        if cnt is None or cnt <= 0:
            return
        h = self.sems
        self.emit(eng, lambda E, s=sem, c=cnt: E.wait_ge(h[s], c))

    def op(self, eng, fn, inc=None, amt=1):
        if inc is None:
            self.emit(eng, fn)
            return None
        self.sc[inc] += amt
        cnt = self.sc[inc]
        h = self.sems
        self.emit(eng, lambda E, f=fn, s=inc, a=amt: f(E).then_inc(h[s], a))
        return cnt

    def dma(self, eng, sem, dst, src):
        return self.op(eng, lambda E, d=dst, s=src: E.dma_start(out=d, in_=s),
                       inc=sem, amt=16)


class Banks:
    def __init__(self, prog):
        self.p = prog
        self.state = {i: None for i in range(8)}

    def acquire(self, i):
        st = self.state[i]
        if st is not None:
            self.p.wait("tensor", st[0], st[1])
        self.state[i] = None

    def release(self, i, sem, cnt):
        self.state[i] = (sem, cnt)


class Arena:
    def __init__(self, handle, nbytes):
        self.h = handle
        self.nbytes = nbytes
        self.off = 0

    def reset(self, off=0):
        self.off = off

    def alloc(self, nbytes, align=64):
        o = (self.off + align - 1) // align * align
        self.off = o + nbytes
        assert self.off <= self.nbytes, f"arena overflow: {self.off} > {self.nbytes}"
        return o

    def f32(self, off, cols, rows=128, row0=0):
        assert off % 4 == 0
        return self.h[row0:row0 + rows, off // 4: off // 4 + cols]

    def bf(self, off, cols, rows=128, row0=0):
        assert off % 2 == 0
        return self.h[:].bitcast(BF16)[row0:row0 + rows, off // 2: off // 2 + cols]


DBG_ATTN = False


def build_program(upto="all"):
    nc = bass.Bass("TRN2", target_bir_lowering=False)

    xio = nc.dram_tensor("x", [D, T], F32, kind="ExternalInput").ap()
    xout = nc.dram_tensor("xout", [D, T], F32, kind="ExternalOutput").ap()

    wqkv = [nc.dram_tensor(f"wqkv{l}", [D, 3 * D], BF16, kind="ExternalInput").ap() for l in (0, 1)]
    wout = [nc.dram_tensor(f"wout{l}", [D, D], BF16, kind="ExternalInput").ap() for l in (0, 1)]
    gwTd = nc.dram_tensor("gwT0", [D, INTER0], BF16, kind="ExternalInput").ap()
    uwTd = nc.dram_tensor("uwT0", [D, INTER0], BF16, kind="ExternalInput").ap()
    dwTd = nc.dram_tensor("dwT0", [INTER0, D], BF16, kind="ExternalInput").ap()
    mgw = nc.dram_tensor("mgw", [E1, D, INTER1], BF16, kind="ExternalInput")
    muw = nc.dram_tensor("muw", [E1, D, INTER1], BF16, kind="ExternalInput")
    mdw = nc.dram_tensor("mdw", [E1, INTER1, D], BF16, kind="ExternalInput")

    c_id = nc.dram_tensor("c_id", [128, 128], BF16, kind="ExternalInput").ap()
    c_idf = nc.dram_tensor("c_idf", [128, 128], F32, kind="ExternalInput").ap()
    c_mask = nc.dram_tensor("c_mask", [128, 1024], BF16, kind="ExternalInput").ap()
    c_iota = nc.dram_tensor("c_iota", [128, 8], F32, kind="ExternalInput").ap()
    c_onesbf = nc.dram_tensor("c_onesbf", [128, 1], BF16, kind="ExternalInput").ap()
    c_onesf = nc.dram_tensor("c_onesf", [1, 128], F32, kind="ExternalInput").ap()
    c_gs = nc.dram_tensor("c_gs", [128, IC0], F32, kind="ExternalInput").ap()
    c_us = nc.dram_tensor("c_us", [128, IC0], F32, kind="ExternalInput").ap()
    c_ds = nc.dram_tensor("c_ds", [128, KCN], F32, kind="ExternalInput").ap()
    c_bq = nc.dram_tensor("c_bq", [128, KCN], F32, kind="ExternalInput").ap()
    c_bv = nc.dram_tensor("c_bv", [128, KCN], F32, kind="ExternalInput").ap()
    c_bp = nc.dram_tensor("c_bp", [128, KCN], F32, kind="ExternalInput").ap()
    c_rwt = nc.dram_tensor("c_rwt", [128, KCN * E1], F32, kind="ExternalInput").ap()

    dbg_xbf = nc.dram_tensor("dbg_xbf", [D, T], BF16, kind="ExternalOutput").ap()
    dbg_row = nc.dram_tensor("dbg_row", [2, 1024], F32, kind="ExternalOutput").ap()
    dbg_qt = nc.dram_tensor("dbg_qt", [D, T], BF16, kind="ExternalOutput").ap()
    dbg_p = nc.dram_tensor("dbg_p", [128, 10240], BF16, kind="ExternalOutput").ap()
    dbg_sst = nc.dram_tensor("dbg_sst", [128, 4096], F32, kind="ExternalOutput").ap()
    dbg_st2 = nc.dram_tensor("dbg_st2", [128, 16], F32, kind="ExternalOutput").ap()
    dbg_kb = nc.dram_tensor("dbg_kb", [128, 4096], BF16, kind="ExternalOutput").ap()
    dbg_vb = nc.dram_tensor("dbg_vb", [128, 4096], BF16, kind="ExternalOutput").ap()
    dbg_ath = nc.dram_tensor("dbg_ath", [128, 512], BF16, kind="ExternalOutput").ap()
    dbg_selt = nc.dram_tensor("dbg_selt", [8, 1024], F32, kind="ExternalOutput").ap()
    dbg_wb = nc.dram_tensor("dbg_wb", [16, 1024], F32, kind="ExternalOutput").ap()
    dbg_ccin = nc.dram_tensor("dbg_ccin", [2 * D, T], BF16, kind="ExternalOutput").ap()
    dbg_ccout = nc.dram_tensor("dbg_ccout", [2 * 2 * D, T], BF16, kind="ExternalOutput").ap()
    cc_in = [nc.dram_tensor(f"cc_in{l}", [2 * D, T], BF16) for l in (0, 1)]
    rstd_scr = nc.dram_tensor("rstd_scr", [1, 1024], F32)
    cc_out = [nc.dram_tensor(f"cc_out{l}", [NCORES * 2 * D, T], BF16) for l in (0, 1)]

    prog = Prog()
    for s in ("ld", "kv", "gd", "pe", "dv", "ac", "cc0", "cc1"):
        prog.sem(s)
    banks = Banks(prog)

    ARN = 33500

    es = contextlib.ExitStack()
    with es:
        XT = es.enter_context(nc.sbuf_tensor("XT", [128, KCN * T], F32))
        ARH = es.enter_context(nc.sbuf_tensor("AR", [128, ARN], F32))
        AR = Arena(ARH, ARN * 4)
        S_ID = es.enter_context(nc.sbuf_tensor("s_id", [128, 128], BF16))
        S_IDF = es.enter_context(nc.sbuf_tensor("s_idf", [128, 128], F32))
        S_MASK = es.enter_context(nc.sbuf_tensor("s_mask", [128, 1024], BF16))
        S_IOTA = es.enter_context(nc.sbuf_tensor("s_iota", [128, 8], F32))
        S_ONESBF = es.enter_context(nc.sbuf_tensor("s_onesbf", [128, 1], BF16))
        S_ONESF = es.enter_context(nc.sbuf_tensor("s_onesf", [1, 128], F32))
        S_GS = es.enter_context(nc.sbuf_tensor("s_gs", [128, IC0], F32))
        S_US = es.enter_context(nc.sbuf_tensor("s_us", [128, IC0], F32))
        S_DS = es.enter_context(nc.sbuf_tensor("s_ds", [128, KCN], F32))
        S_BQ = es.enter_context(nc.sbuf_tensor("s_bq", [128, KCN], F32))
        S_BV = es.enter_context(nc.sbuf_tensor("s_bv", [128, KCN], F32))
        S_BP = es.enter_context(nc.sbuf_tensor("s_bp", [128, KCN], F32))
        S_RWT = es.enter_context(nc.sbuf_tensor("s_rwt", [128, KCN * E1], F32))
        S_ROWS = es.enter_context(nc.sbuf_tensor("s_rows", [128, 2048], F32))
        PS = [es.enter_context(nc.psum_tensor(f"ps{i}", [128, 512], F32)) for i in range(8)]
        for s in list(prog.sc):
            prog.sems[s] = es.enter_context(nc.semaphore(s))

        def psf(i):
            return PS[i][:]

        def psbf(i):
            return PS[i][:].bitcast(BF16)

        MR = S_ROWS[0:1, 0:1024]
        RSTDR = S_ROWS[0:1, 1024:2048]
        RSTDC = S_ROWS[:, 2048 - 8:2048]
        RSTDC = None

        S_RC = es.enter_context(nc.sbuf_tensor("s_rc", [128, 16], F32))
        RSTDC = S_RC[:, 0:8]
        PRIME_IN = S_RC[:, 8:9]
        PRIME_OUT = S_RC[:, 9:10]

        ld_x = prog.dma("sync", "ld",
                        XT[:].rearrange("p (kc t) -> p kc t", kc=KCN),
                        xio.rearrange("(kc p) t -> p kc t", p=128))
        ld_consts = ld_x
        for dst, src in ((S_ID[:], c_id), (S_IDF[:], c_idf), (S_MASK[:], c_mask),
                         (S_IOTA[:], c_iota), (S_ONESBF[:], c_onesbf), (S_ONESF[:], c_onesf),
                         (S_GS[:], c_gs), (S_US[:], c_us), (S_DS[:], c_ds),
                         (S_BQ[:], c_bq), (S_BV[:], c_bv), (S_BP[:], c_bp),
                         (S_RWT[:], c_rwt)):
            ld_consts = prog.dma("sync", "ld", dst, src)

        prog.wait("scalar", "ld", ld_consts)
        prog.op("scalar", lambda E: E.activation(out=PRIME_IN, in_=S_DS[:, 0:1],
                                                 func=AF.Exp, accum_out=PRIME_OUT), inc="ac")

        def emit_ln(prev_dv):
            AR.reset()
            xbf_o = AR.alloc(2 * KCN * T)
            xsq_o = AR.alloc(2 * KCN * T)
            row_o = AR.alloc(4 * 4096)
            scr_o = AR.alloc(4 * 512)
            XBF = AR.bf(xbf_o, KCN * T)
            XSQ = AR.bf(xsq_o, KCN * T)
            SUMR = AR.f32(row_o, 1024, rows=1)
            SQR = AR.f32(row_o + 4096, 1024, rows=1)
            VARR = AR.f32(row_o + 8192, 1024, rows=1)
            STDR = AR.f32(row_o + 12288, 1024, rows=1)
            SCR = AR.f32(scr_o, 512)

            prog.wait("vector", "ld", ld_x)
            c1 = prog.op("vector", lambda E: E.tensor_copy(out=XBF, in_=XT[:]), inc="dv")
            prog.wait("scalar", "ld", ld_x)
            prog.wait("scalar", "dv", prev_dv)
            c2 = prog.op("scalar", lambda E: E.activation(out=XSQ, in_=XT[:], func=AF.Square),
                         inc="ac")
            prog.wait("tensor", "dv", c1)
            prog.wait("tensor", "ac", c2)

            for half in (0, 1):
                pecnt = {}
                for q in (0, 1):
                    bank = q
                    banks.acquire(bank)
                    src = XBF if q == 0 else XSQ
                    for kc in range(KCN):
                        rhs = src[:, kc * T + half * 512: kc * T + half * 512 + 512]
                        pecnt[q] = prog.op(
                            "tensor",
                            lambda E, o=PS[bank][0:1, 0:512], r=rhs, st=(kc == 0), sp=(kc == KCN - 1):
                            E.matmul(out=o, lhsT=S_ONESBF[:], rhs=r, start=st, stop=sp),
                            inc=("pe" if kc == KCN - 1 else None))
                for q, dstrow in ((0, SUMR), (1, SQR)):
                    prog.wait("vector", "pe", pecnt[q])
                    ev = prog.op("vector",
                                 lambda E, o=dstrow[0:1, half * 512:half * 512 + 512],
                                 i=PS[q][0:1, 0:512]: E.tensor_copy(out=o, in_=i), inc="dv")
                    banks.release(q, "dv", ev)

            prog.op("vector", lambda E: E.tensor_scalar(out=MR, in0=SUMR, scalar1=1.0 / D,
                                                        scalar2=None, op0=ALU.mult))
            prog.op("vector", lambda E: E.tensor_scalar(out=SQR, in0=SQR, scalar1=1.0 / D,
                                                        scalar2=EPS, op0=ALU.mult, op1=ALU.add))
            prog.op("vector", lambda E: E.tensor_tensor(out=VARR, in0=MR, in1=MR, op=ALU.mult))
            dv_c = prog.op("vector", lambda E: E.tensor_tensor(out=VARR, in0=SQR, in1=VARR,
                                                               op=ALU.subtract), inc="dv")
            prog.wait("scalar", "dv", dv_c)
            ac_c = prog.op("scalar", lambda E: E.activation(out=STDR, in_=VARR, func=AF.Sqrt),
                           inc="ac")
            prog.wait("vector", "ac", ac_c)
            dv_c = prog.op("vector", lambda E: E.reciprocal(out=RSTDR, in_=STDR), inc="dv")
            prog.wait("tensor", "dv", dv_c)

            last = None
            for half in (0, 1):
                banks.acquire(2)
                banks.acquire(3)
                for bank, row in ((2, MR), (3, RSTDR)):
                    pe_c = prog.op("tensor",
                                   lambda E, o=psf(bank), r=row[0:1, half * 512:half * 512 + 512]:
                                   E.matmul(out=o, lhsT=S_ONESF[:], rhs=r, start=True, stop=True),
                                   inc="pe")
                prog.wait("vector", "pe", pe_c)
                for kc in range(KCN):
                    sl = slice(kc * T + half * 512, kc * T + half * 512 + 512)
                    prog.op("vector", lambda E, o=SCR, a=XT[:, sl], b=psf(2):
                            E.tensor_tensor(out=o, in0=a, in1=b, op=ALU.subtract))
                    last = prog.op("vector", lambda E, o=XBF[:, sl], a=SCR, b=psf(3):
                                   E.tensor_tensor(out=o, in0=a, in1=b, op=ALU.mult),
                                   inc=("dv" if kc == KCN - 1 else None))
                banks.release(2, "dv", last)
                banks.release(3, "dv", last)
            return xbf_o, last

        def emit_qkv(layer, xbf_o, xbf_ready):
            XBF = AR.bf(xbf_o, KCN * T)
            AR.reset(2 * KCN * T)
            qt_o = AR.alloc(2 * KCN * T)
            wblk_o = AR.alloc(2 * KCN * 256 * 2)
            vw_o = AR.alloc(2 * KCN * 512 * 2)
            kst_o = AR.alloc(2 * T * 2)
            vst_o = AR.alloc(2 * 512 * 2)
            QTv = AR.bf(qt_o, KCN * T)

            wq3 = wqkv[layer].rearrange("(kc p) n -> p kc n", p=128)
            ccsem = "cc%d" % layer
            prog.wait("tensor", "dv", xbf_ready)
            prog.wait("sync", "dv", xbf_ready)
            prog.wait("sync", "pe", prog.sc["pe"])
            state = {}

            def proj_fm(ob0, nobs, kind):
                for ip in range(nobs // 2):
                    buf = ip % 2
                    obA = ob0 + 2 * ip
                    wdst = AR.bf(wblk_o + buf * KCN * 256 * 2, KCN * 256)
                    key = ("wblkld", buf)
                    if ("wblk_prev",) in state:
                        prog.wait("sync", "pe", state[("wblk_prev",)])
                    lc = prog.dma("sync", "ld",
                                  wdst.rearrange("p (kc n) -> p kc n", kc=KCN),
                                  wq3[:, :, obA * 128:(obA + 2) * 128])
                    pe_c = None
                    for sub in range(2):
                        ob = obA + sub
                        prog.wait("tensor", "ld", lc)
                        for half in (0, 1):
                            bank = 4 + (2 * (2 * ip + sub) + half) % 2
                            banks.acquire(bank)
                            for kc in range(KCN):
                                lhs = wdst[:, kc * 256 + sub * 128: kc * 256 + sub * 128 + 128]
                                rhs = XBF[:, kc * T + half * 512: kc * T + half * 512 + 512]
                                pe_c = prog.op(
                                    "tensor",
                                    lambda E, o=psf(bank), l=lhs, r=rhs, st=(kc == 0),
                                    sp=(kc == KCN - 1):
                                    E.matmul(out=o, lhsT=l, rhs=r, start=st, stop=sp),
                                    inc=("pe" if kc == KCN - 1 else None))
                            prog.wait("vector", "pe", pe_c)
                            if kind == "K":
                                kdst = AR.bf(kst_o + buf * T * 2, T)[:, half * 512:half * 512 + 512]
                                if half == 0 and ("kst", buf) in state:
                                    prog.wait("vector", "gd", state[("kst", buf)])
                                ev = prog.op("vector", lambda E, o=kdst, i=psf(bank):
                                             E.tensor_copy(out=o, in_=i), inc="dv")
                            else:
                                qdst = QTv[:, ob * T + half * 512: ob * T + half * 512 + 512]
                                if layer == 1:
                                    ev = prog.op("vector",
                                                 lambda E, o=qdst, i=psf(bank), b=S_BQ[:, ob:ob + 1]:
                                                 E.tensor_scalar(out=o, in0=i, scalar1=b,
                                                                 scalar2=None, op0=ALU.add),
                                                 inc="dv")
                                else:
                                    ev = prog.op("vector", lambda E, o=qdst, i=psf(bank):
                                                 E.tensor_copy(out=o, in_=i), inc="dv")
                            banks.release(bank, "dv", ev)
                        if kind == "K":
                            prog.wait("gpsimd", "dv", ev)
                            g = prog.dma("gpsimd", "gd",
                                         cc_in[layer].ap()[(ob - 16) * 128:(ob - 15) * 128, :],
                                         AR.bf(kst_o + buf * T * 2, T))
                            state[("kst", buf)] = g
                    state[("wblkld", buf)] = pe_c
                    state[("wblk_prev",)] = pe_c

            proj_fm(16, 16, "K")

            for dg in range(4):
                buf = dg % 2
                wdst = AR.bf(vw_o + buf * KCN * 512 * 2, KCN * 512)
                key = ("vld", buf)
                if key in state:
                    prog.wait("sync", "pe", state[key])
                lc = prog.dma("sync", "ld",
                              wdst.rearrange("p (kc n) -> p kc n", kc=KCN),
                              wq3[:, :, 4096 + dg * 512: 4096 + (dg + 1) * 512])
                prog.wait("tensor", "ld", lc)
                pe_c = None
                for tt in range(8):
                    bank = 4 + tt % 2
                    banks.acquire(bank)
                    for kc in range(KCN):
                        lhs = XBF[:, kc * T + tt * 128: kc * T + tt * 128 + 128]
                        rhs = wdst[:, kc * 512:(kc + 1) * 512]
                        pe_c = prog.op("tensor",
                                       lambda E, o=psf(bank), l=lhs, r=rhs, st=(kc == 0),
                                       sp=(kc == KCN - 1):
                                       E.matmul(out=o, lhsT=l, rhs=r, start=st, stop=sp),
                                       inc=("pe" if kc == KCN - 1 else None))
                    sb = tt % 2
                    vdst = AR.bf(vst_o + sb * 512 * 2, 512)
                    prog.wait("vector", "pe", pe_c)
                    if ("vst", sb) in state:
                        prog.wait("vector", "gd", state[("vst", sb)])
                    ev = prog.op("vector", lambda E, o=vdst, i=psf(bank):
                                 E.tensor_copy(out=o, in_=i), inc="dv")
                    banks.release(bank, "dv", ev)
                    prog.wait("gpsimd", "dv", ev)
                    dst_ap = bass.AP(cc_in[layer],
                                     (D + 2 * tt * 128 + dg // 2) * T + (dg % 2) * 512,
                                     [[2 * T, 128], [1, 512]])
                    g = prog.dma("gpsimd", "gd", dst_ap, vdst)
                    state[("vst", sb)] = g
                state[("vld", buf)] = pe_c

            prog.wait("gpsimd", "gd", prog.sc["gd"])
            prog.op("gpsimd", lambda E, l=layer: E.collective_compute(
                "AllGather", ALU.bypass, replica_groups=[list(range(NCORES))],
                ins=[cc_in[l].ap().opt()], outs=[cc_out[l].ap().opt()]), inc=ccsem)

            proj_fm(0, 16, "Q")
            return qt_o, prog.sc["dv"]

        def emit_attention(layer, qt_o, q_ready):
            QTv = AR.bf(qt_o, KCN * T)
            AR.reset(4 * KCN * T)
            p_o = [AR.alloc(2 * 1024 * (k + 1)) for k in range(4)]
            s_o = AR.alloc(4 * 4096)
            kbuf_o = [AR.alloc(2 * 4096) for _ in range(2)]
            vbuf_o = AR.alloc(2 * 4096)
            pt_o = AR.alloc(4 * 512 * 2)
            stat_o = AR.alloc(4 * 64)
            slack_o = AR.alloc(4 * 512)
            attnT_o = 0
            wo_o = s_o
            SST = AR.f32(s_o, 4096)
            MXA = AR.f32(stat_o, 32)
            MCOLS = AR.f32(stat_o + 128, 4)
            NEGBS = AR.f32(stat_o + 144, 4)
            LCOLS = AR.f32(stat_o + 160, 4)
            RLS = AR.f32(stat_o + 176, 4)
            ATT = AR.bf(attnT_o, KCN * 512)
            SLACK = AR.f32(slack_o, 512)
            ccsem = "cc%d" % layer
            wo3 = wout[layer].rearrange("(kc p) n -> p kc n", p=128)

            prog.wait("scalar", ccsem, 1)
            prog.wait("tensor", "dv", q_ready)

            kvfree = {0: None, 1: None}
            kv_done = {}

            def issue_kv(s, h, slot):
                if kvfree[slot] is not None:
                    prog.wait("scalar", "pe", kvfree[slot])
                KB4 = AR.bf(kbuf_o[slot], 4096).rearrange("p (kb o jj) -> p kb o jj", o=8, jj=64)
                VB4 = AR.bf(vbuf_o, 4096).rearrange("p (kb q d) -> p kb q d", q=4, d=128)
                c = None
                for o in range(NCORES):
                    srck = bass.AP(cc_out[layer], (o * 2 * D + h * 128) * T + s * 512,
                                   [[T, 128], [64, 8], [1, 1], [1, 64]])
                    c = prog.dma("scalar", "kv", KB4[:, :, o:o + 1, :], srck)
                if s is not None and True:
                    pass
                return c

            def issue_v(s, h, after_pe):
                if after_pe is not None:
                    prog.wait("scalar", "pe", after_pe)
                VB4 = AR.bf(vbuf_o, 4096).rearrange("p (kb q d) -> p kb q d", q=4, d=128)
                c = None
                for o in range(NCORES):
                    srcv = bass.AP(cc_out[layer],
                                   (o * 2 * D + D + 2 * (s * 512) + h // 8) * T + (h % 8) * 128,
                                   [[2 * T, 64], [128 * T, 8], [1, 1], [1, 128]])
                    dstv = VB4[64 * (o % 2):64 * (o % 2) + 64, :, o // 2:o // 2 + 1, :]
                    c = prog.dma("scalar", "kv", dstv, srcv)
                return c

            order = [(s, h) for s in (0, 1) for h in range(H)]
            kslot = {}
            kv_k = {}
            kv_v = {}
            kslot[order[0]] = 0
            kv_k[order[0]] = issue_kv(*order[0], 0)
            kv_v[order[0]] = issue_v(*order[0], None)

            prev_pv = None
            prev_x3 = [None]
            head_done = {}
            prog.op("vector", lambda E: E.memset(NEGBS[:, 0:4], 0.0))
            wo_free = {}
            last_exp_of_chunk = {0: None, 1: None}

            for idx, (s, h) in enumerate(order):
                slot = kslot[(s, h)]
                if idx + 1 < len(order):
                    kslot[order[idx + 1]] = 1 - slot
                    kv_k[order[idx + 1]] = issue_kv(*order[idx + 1], 1 - slot)
                KB = AR.bf(kbuf_o[slot], 4096)
                VB = AR.bf(vbuf_o, 4096)
                prog.wait("tensor", "kv", kv_k[(s, h)])

                pn = {}
                x_c = {}
                dve_n = [0]

                def dop(fn, inc=None):
                    dve_n[0] += 1
                    return prog.op("vector", fn, inc=inc)

                def pad_to(target):
                    while dve_n[0] < target:
                        dop(lambda E: E.tensor_copy(out=SLACK[:, 0:64], in_=SST[:, 0:64]))

                pos = {}

                def stage_E(k):
                    nb = 2 * (k + 1)
                    for kb in range(nb):
                        bank = (0, 1, 4, 5)[kb % 4]
                        banks.acquire(bank)
                        lhs = QTv[:, h * T + s * 512 + k * 128: h * T + s * 512 + k * 128 + 128]
                        rhs = KB[:, kb * 512:(kb + 1) * 512]
                        pc = prog.op("tensor", lambda E, o=psf(bank), l=lhs, r=rhs:
                                     E.matmul(out=o, lhsT=l, rhs=r, start=True, stop=True),
                                     inc="pe")
                        prog.wait("vector", "pe", pc)
                        if kb == 0:
                            if k >= 1:
                                prog.wait("vector", "ac", x_c[k - 1])
                            elif prev_x3[0] is not None:
                                prog.wait("vector", "ac", prev_x3[0])
                        dst = SST[:, kb * 512:(kb + 1) * 512]
                        if kb < nb - 2:
                            ev = dop(lambda E, o=dst, i=psf(bank), m=MXA[:, 8 * k + kb:8 * k + kb + 1]:
                                     E.tensor_scalar(out=o, in0=i, scalar1=1.0, scalar2=None,
                                                     op0=ALU.mult, op1=ALU.max, accum_out=m),
                                     inc="dv")
                        else:
                            mi = kb - (nb - 2)
                            dop(lambda E, o=dst, i=psf(bank),
                                mk=S_MASK[:, mi * 512:(mi + 1) * 512]:
                                E.scalar_tensor_tensor(out=o, in0=i, scalar=1.0, in1=mk,
                                                       op0=ALU.mult, op1=ALU.add))
                            ev = dop(lambda E, o=MXA[:, 8 * k + kb:8 * k + kb + 1], i=dst:
                                     E.reduce_max(out=o, in_=i, axis=AX.X), inc="dv")
                        banks.release(bank, "dv", ev)

                def stage_M(k):
                    nb = 2 * (k + 1)
                    dop(lambda E, o=MCOLS[:, k:k + 1], i=MXA[:, 8 * k:8 * k + nb]:
                        E.reduce_max(out=o, in_=i, axis=AX.X))
                    pos[("M", k)] = dve_n[0]

                def stage_N(k):
                    pad_to(pos[("M", k)] + 6)
                    c = dop(lambda E, o=NEGBS[:, k:k + 1], i=MCOLS[:, k:k + 1]:
                            E.tensor_scalar(out=o, in0=i, scalar1=-CSC, scalar2=None,
                                            op0=ALU.mult), inc="dv")
                    pos[("N", k)] = c

                def stage_X(k):
                    nb = 2 * (k + 1)
                    Pk = AR.bf(p_o[k], 1024 * (k + 1))
                    prog.wait("scalar", "dv", pos[("N", k)])
                    if prev_pv is not None:
                        prog.wait("scalar", "pe", prev_pv)
                    x_c[k] = prog.op("scalar", lambda E, o=Pk, i=SST[:, 0:nb * 512],
                                     b=NEGBS[:, k:k + 1]:
                                     E.activation(out=o, in_=i, func=AF.Exp, scale=CSC,
                                                  bias=b), inc="ac")
                    last_exp_of_chunk[s] = x_c[k]
                    if k == 3:
                        prev_x3[0] = x_c[k]

                def stage_L(k):
                    Pk = AR.bf(p_o[k], 1024 * (k + 1))
                    prog.wait("vector", "ac", x_c[k])
                    dop(lambda E, o=SLACK[:, 0:64], i=SST[:, 0:64]: E.tensor_copy(out=o, in_=i))
                    dop(lambda E, o=LCOLS[:, k:k + 1], i=Pk:
                        E.reduce_sum(out=o, in_=i, axis=AX.X))
                    pos[("L", k)] = dve_n[0]

                def stage_R(k):
                    pad_to(pos[("L", k)] + 6)
                    dop(lambda E, o=RLS[:, k:k + 1], i=LCOLS[:, k:k + 1]:
                        E.reciprocal(out=o, in_=i))
                    pos[("R", k)] = dve_n[0]

                def stage_B(k):
                    pad_to(pos[("R", k)] + 8)
                    Pk = AR.bf(p_o[k], 1024 * (k + 1))
                    pn[k] = dop(lambda E, o=Pk, i=Pk, r=RLS[:, k:k + 1]:
                                E.tensor_scalar(out=o, in0=i, scalar1=r, scalar2=None,
                                                op0=ALU.mult), inc="dv")

                for g in range(7):
                    if g <= 3:
                        stage_E(g)
                        stage_M(g)
                    if 0 <= g - 1 <= 3:
                        stage_L(g - 1)
                    if 0 <= g - 2 <= 3:
                        stage_R(g - 2)
                    if 0 <= g - 3 <= 3:
                        stage_B(g - 3)
                    if g <= 3:
                        stage_N(g)
                        stage_X(g)

                attn_bank = 6 + idx % 2
                banks.acquire(attn_bank)
                for k in range(4):
                    prog.wait("tensor", "dv", pn[k])
                prog.wait("tensor", "kv", kv_v[(s, h)])
                pv_c = None
                ptev = {}
                NPT = 4
                def emit_tr(kt):
                    start_q = kt // 8
                    nq = 4 - start_q
                    tb = 2 + kt % 2
                    banks.acquire(tb)
                    tr_c = None
                    for qi in range(start_q, 4):
                        Pq = AR.bf(p_o[qi], 1024 * (qi + 1))
                        tr_c = prog.op(
                            "tensor",
                            lambda E, o=psbf(tb)[:, (qi - start_q) * 128:(qi - start_q) * 128 + 128],
                            i=Pq[:, kt * 128:kt * 128 + 128]:
                            E.transpose(out=o, in_=i, identity=S_ID[:]),
                            inc=("pe" if qi == 3 else None))
                    ptbuf = AR.bf(pt_o + (kt % NPT) * 1024, nq * 128)
                    prog.wait("scalar", "pe", tr_c)
                    if kt >= NPT:
                        prog.wait("scalar", "pe", pvs[kt - NPT])
                    pv = prog.op("scalar", lambda E, o=ptbuf, i=psbf(tb)[:, 0:nq * 128]:
                                 E.activation(out=o, in_=i, func=AF.Copy), inc="ac")
                    banks.release(tb, "ac", pv)
                    ptev[kt] = (pv, ptbuf)
                pvs = {}
                emit_tr(0)
                emit_tr(1)
                emit_tr(2)
                for kt in range(32):
                    if kt + 3 < 32:
                        emit_tr(kt + 3)
                    start_q = kt // 8
                    pev, ptbuf = ptev[kt]
                    prog.wait("tensor", "ac", pev)
                    pv_c = prog.op("tensor",
                                   lambda E, o=psf(attn_bank)[:, start_q * 128:512],
                                   l=VB[:, kt * 128:kt * 128 + 128], r=ptbuf,
                                   st=(kt == 0), sp=(kt == 31):
                                   E.matmul(out=o, lhsT=l, rhs=r, start=st, stop=sp),
                                   inc=("pe" if kt == 31 else None))
                    pvs[kt] = prog.sc["pe"] if pv_c is None else pv_c
                prev_pv = pv_c
                kvfree[slot] = pv_c
                prog.wait("vector", "pe", pv_c)
                if layer == 1:
                    aev = prog.op("vector",
                                  lambda E, o=ATT[:, h * 512:(h + 1) * 512], i=psf(attn_bank),
                                  b=S_BV[:, h:h + 1]:
                                  E.tensor_scalar(out=o, in0=i, scalar1=b, scalar2=None,
                                                  op0=ALU.add), inc="dv")
                else:
                    aev = prog.op("vector",
                                  lambda E, o=ATT[:, h * 512:(h + 1) * 512], i=psf(attn_bank):
                                  E.tensor_copy(out=o, in_=i), inc="dv")
                banks.release(attn_bank, "dv", aev)
                head_done[(s, h)] = aev
                if DBG_ATTN and layer == 0 and s == 0 and h == 0:
                    prog.wait("gpsimd", "dv", aev)
                    prog.dma("gpsimd", "gd", dbg_p[:, 0:1024], AR.bf(p_o[0], 1024))
                    prog.dma("gpsimd", "gd", dbg_p[:, 1024:3072], AR.bf(p_o[1], 2048))
                    prog.dma("gpsimd", "gd", dbg_p[:, 3072:6144], AR.bf(p_o[2], 3072))
                    prog.dma("gpsimd", "gd", dbg_p[:, 6144:10240], AR.bf(p_o[3], 4096))
                    prog.dma("gpsimd", "gd", dbg_sst[:, :], SST)
                    prog.dma("gpsimd", "gd", dbg_st2[:, :], AR.f32(stat_o, 16))
                    prog.dma("gpsimd", "gd", dbg_kb[:, :], AR.bf(kbuf_o[slot], 4096))
                    prog.dma("gpsimd", "gd", dbg_vb[:, :], AR.bf(vbuf_o, 4096))
                    dbgc = prog.dma("gpsimd", "gd", dbg_ath[:, :], ATT[:, 0:512])
                    prog.wait("vector", "gd", dbgc)
                    prog.wait("tensor", "gd", dbgc)
                    prog.wait("scalar", "gd", dbgc)
                if idx + 1 < len(order):
                    prog.wait("scalar", "dv", aev)
                    kv_v[order[idx + 1]] = issue_v(*order[idx + 1], None)

                if h == H - 1:
                    wo_pe = {}
                    for oc in range(KCN):
                        buf = oc % 2
                        wdst = AR.bf(wo_o + buf * KCN * 128 * 2, KCN * 128)
                        if oc < 2:
                            prog.wait("sync", "ac", last_exp_of_chunk[s])
                        if oc >= 1 and ("wores",) in wo_pe:
                            prog.wait("sync", "dv", wo_pe[("wores",)])
                        lc = prog.dma("sync", "ld",
                                      wdst.rearrange("p (kc n) -> p kc n", kc=KCN),
                                      wo3[:, :, oc * 128:(oc + 1) * 128])
                        bank = 2 + oc % 2
                        banks.acquire(bank)
                        prog.wait("tensor", "ld", lc)
                        if oc == 0:
                            for hh in range(H):
                                prog.wait("tensor", "dv", head_done[(s, hh)])
                        pc = None
                        for kc in range(KCN):
                            pc = prog.op("tensor",
                                         lambda E, o=psf(bank), l=wdst[:, kc * 128:(kc + 1) * 128],
                                         r=ATT[:, kc * 512:(kc + 1) * 512],
                                         st=(kc == 0), sp=(kc == KCN - 1):
                                         E.matmul(out=o, lhsT=l, rhs=r, start=st, stop=sp),
                                         inc=("pe" if kc == KCN - 1 else None))
                        wo_pe[oc] = pc
                        prog.wait("vector", "pe", pc)
                        xsl = XT[:, oc * T + s * 512: oc * T + s * 512 + 512]
                        bias = S_BP[:, oc:oc + 1] if layer == 1 else 0.0
                        rc = prog.op("vector",
                                     lambda E, o=xsl, i=psf(bank), b=bias, x=xsl:
                                     E.scalar_tensor_tensor(out=o, in0=i, scalar=b, in1=x,
                                                            op0=ALU.add, op1=ALU.add), inc="dv")
                        banks.release(bank, "dv", rc)
                        wo_pe[("wores",)] = rc
            return prog.sc["dv"]

        def emit_ffn0(xbf_o):
            XBF = AR.bf(xbf_o, KCN * T)
            AR.reset(2 * KCN * T)
            h_o = AR.alloc(2 * IC0 * 512)
            gw_o = AR.alloc(2 * KCN * 256 * 2)
            uw_o = AR.alloc(2 * KCN * 256 * 2)
            dw_o = AR.alloc(2 * 22 * 128 * 2)
            ga_o = AR.alloc(4 * 512 * 2)

            prog.wait("sync", "dv", prog.sc["dv"])
            prog.wait("sync", "pe", prog.sc["pe"])
            gw3 = gwTd.rearrange("(kc p) n -> p kc n", p=128)
            uw3 = uwTd.rearrange("(kc p) n -> p kc n", p=128)
            dw3 = dwTd.rearrange("(ic p) n -> p ic n", p=128)

            state = {}

            def _emit_h(state, key):
                sc_, gact, ubank, icx = state.pop(key)
                prog.wait("vector", "ac", sc_)
                hdst = AR.bf(h_o + icx * 1024, 512)
                hv = prog.op("vector",
                             lambda E, o=hdst, i=psf(ubank), s=S_US[:, icx:icx + 1], g=gact:
                             E.scalar_tensor_tensor(out=o, in0=i, scalar=s, in1=g,
                                                    op0=ALU.mult, op1=ALU.mult), inc="dv")
                banks.release(ubank, "dv", hv)
                state[("h", icx)] = hv

            for tt in range(2):
                if tt == 1 and ("downlast",) in state:
                    pass
                for ipr in range(IC0 // 2):
                    buf = ipr % 2
                    gwb = AR.bf(gw_o + buf * KCN * 256 * 2, KCN * 256)
                    uwb = AR.bf(uw_o + buf * KCN * 256 * 2, KCN * 256)
                    if ("gld_prev",) in state:
                        prog.wait("sync", "pe", state[("gld_prev",)])
                    prog.dma("sync", "ld", gwb.rearrange("p (kc n) -> p kc n", kc=KCN),
                             gw3[:, :, ipr * 256:(ipr + 1) * 256])
                    lu = prog.dma("sync", "ld", uwb.rearrange("p (kc n) -> p kc n", kc=KCN),
                                  uw3[:, :, ipr * 256:(ipr + 1) * 256])
                    pe_last = None
                    for sub in range(2):
                        ic = 2 * ipr + sub
                        gbank = (ic % 2) * 2
                        ubank = gbank + 1
                        prog.wait("tensor", "ld", lu)
                        if tt == 1 and ipr == 0 and sub == 0:
                            prog.wait("vector", "pe", state[("downlast",)])
                        banks.acquire(gbank)
                        banks.acquire(ubank)
                        pc = None
                        for kc in range(KCN):
                            rhs = XBF[:, kc * T + tt * 512: kc * T + tt * 512 + 512]
                            prog.op("tensor",
                                    lambda E, o=psf(gbank),
                                    l=gwb[:, kc * 256 + sub * 128: kc * 256 + sub * 128 + 128],
                                    r=rhs, st=(kc == 0), sp=(kc == KCN - 1):
                                    E.matmul(out=o, lhsT=l, rhs=r, start=st, stop=sp))
                            pc = prog.op("tensor",
                                         lambda E, o=psf(ubank),
                                         l=uwb[:, kc * 256 + sub * 128: kc * 256 + sub * 128 + 128],
                                         r=rhs, st=(kc == 0), sp=(kc == KCN - 1):
                                         E.matmul(out=o, lhsT=l, rhs=r, start=st, stop=sp),
                                         inc=("pe" if kc == KCN - 1 else None))
                        pe_last = pc
                        prog.wait("scalar", "pe", pc)
                        gact = AR.f32(ga_o + (ic % 2) * 2048, 512)
                        sc_ = prog.op("scalar",
                                      lambda E, o=gact, i=psf(gbank), s=S_GS[:, ic:ic + 1]:
                                      E.activation(out=o, in_=i, func=AF.Silu, scale=s), inc="ac")
                        banks.release(gbank, "ac", sc_)
                        state[("silu", ic)] = (sc_, gact, ubank, ic)
                        if ic >= 1:
                            _emit_h(state, ("silu", ic - 1))
                    state[("gld_prev",)] = pe_last

                _emit_h(state, ("silu", IC0 - 1))
                for hc in range(KCN):
                    ldh = {}
                    for hf in range(2):
                        if ("dres",) in state:
                            prog.wait("sync", "dv", state[("dres",)])
                        ldh[hf] = prog.dma(
                            "sync", "ld",
                            AR.bf(dw_o + hf * 22 * 128 * 2, 22 * 128)
                            .rearrange("p (ic n) -> p ic n", ic=22),
                            dw3[:, hf * 22:(hf + 1) * 22, hc * 128:(hc + 1) * 128])
                    bank = 4 + hc % 2
                    banks.acquire(bank)
                    if hc == 0:
                        for ic in range(IC0):
                            prog.wait("tensor", "dv", state[("h", ic)])
                    pc = None
                    for hf in range(2):
                        prog.wait("tensor", "ld", ldh[hf])
                        for ici in range(22):
                            ic = hf * 22 + ici
                            pc = prog.op("tensor",
                                         lambda E, o=psf(bank),
                                         l=AR.bf(dw_o + hf * 22 * 128 * 2 + ici * 256, 128),
                                         r=AR.bf(h_o + ic * 1024, 512),
                                         st=(ic == 0), sp=(ic == IC0 - 1):
                                         E.matmul(out=o, lhsT=l, rhs=r, start=st, stop=sp),
                                         inc=("pe" if ic == IC0 - 1 else None))
                    prog.wait("vector", "pe", pc)
                    xsl = XT[:, hc * T + tt * 512: hc * T + tt * 512 + 512]
                    rc = prog.op("vector",
                                 lambda E, o=xsl, i=psf(bank), s=S_DS[:, hc:hc + 1], x=xsl:
                                 E.scalar_tensor_tensor(out=o, in0=i, scalar=s, in1=x,
                                                        op0=ALU.mult, op1=ALU.add), inc="dv")
                    banks.release(bank, "dv", rc)
                    state[("downlast",)] = pc
                    state[("dres",)] = rc
            return prog.sc["dv"]

        def emit_moe(xbf_o):
            XBF = AR.bf(xbf_o, KCN * T)
            AR.reset(2 * KCN * T)
            tmp_o = AR.alloc(4 * KCN * 128)
            lgw_o = AR.alloc(4 * 256)
            sel_o = AR.alloc(4 * 1024)
            h_o = AR.alloc(2 * IC1 * T)
            gw_o = AR.alloc(2 * KCN * 256 * 2)
            uw_o = AR.alloc(2 * KCN * 256 * 2)
            dwm_o = AR.alloc(2 * IC1 * 128 * 2)
            wb_o = AR.alloc(4 * T)
            rowb_o = AR.alloc(4 * T)
            t1_o = AR.alloc(4 * 512 * 2)

            prog.wait("sync", "dv", prog.sc["dv"])
            prog.wait("sync", "pe", prog.sc["pe"])
            selT = AR.f32(sel_o, 1024, rows=8)
            WB = AR.f32(wb_o, T)
            ROWB = AR.f32(rowb_o, T, rows=1)

            prog.wait("gpsimd", "dv", prog.sc["dv"])
            rq0 = prog.dma("gpsimd", "gd", rstd_scr.ap()[0:1, :], RSTDR)
            prog.wait("gpsimd", "gd", rq0)
            rq = prog.op("gpsimd", lambda E, d=RSTDC,
                         s=rstd_scr.ap().rearrange("o (t p) -> (o p) t", p=128):
                         E.dma_start(out=d, in_=s, allow_slow_non_contiguous=True),
                         inc="gd", amt=16)

            ROUT = AR.f32(lgw_o, 256)
            def rt(t, a, b):
                return AR.f32(lgw_o + 4 * (32 * t + a), b - a)

            sel_done = None
            for ttile in range(8):
                banks.acquire(2)
                pe_b = prog.op("tensor",
                               lambda E, o=psf(2)[:, 0:128],
                               r=MR[0:1, ttile * 128:(ttile + 1) * 128]:
                               E.matmul(out=o, lhsT=S_ONESF[:], rhs=r, start=True, stop=True),
                               inc="pe")
                prog.wait("vector", "pe", pe_b)
                tc = None
                for kc in range(KCN):
                    tc = prog.op("vector",
                                 lambda E, o=AR.f32(tmp_o + kc * 512, 128),
                                 a=XT[:, kc * T + ttile * 128: kc * T + ttile * 128 + 128],
                                 b=psf(2)[:, 0:128]:
                                 E.tensor_tensor(out=o, in0=a, in1=b, op=ALU.subtract),
                                 inc=("dv" if kc == KCN - 1 else None))
                banks.release(2, "dv", tc)
                prog.wait("tensor", "dv", tc)
                banks.acquire(3)
                pc = None
                for kc in range(KCN):
                    pc = prog.op("tensor",
                                 lambda E, o=PS[3][:, 0:8], l=AR.f32(tmp_o + kc * 512, 128),
                                 r=S_RWT[:, kc * 8:(kc + 1) * 8],
                                 st=(kc == 0), sp=(kc == KCN - 1):
                                 E.matmul(out=o, lhsT=l, rhs=r, start=st, stop=sp),
                                 inc=("pe" if kc == KCN - 1 else None))
                prog.wait("vector", "pe", pc)
                ev = prog.op("vector", lambda E, o=rt(ttile, 0, 8), i=PS[3][:, 0:8]:
                             E.tensor_copy(out=o, in_=i), inc="dv")
                banks.release(3, "dv", ev)

            for t in range(8):
                prog.op("vector", lambda E, o=rt(t, 24, 25), i=rt(t, 0, 8):
                        E.reduce_max(out=o, in_=i, axis=AX.X))
            for t in range(8):
                prog.op("vector", lambda E, o=rt(t, 8, 16), i=rt(t, 0, 8), v=rt(t, 24, 25):
                        E.tensor_scalar(out=o, in0=i, scalar1=v, scalar2=None, op0=ALU.is_ge))
            for t in range(8):
                prog.op("vector", lambda E, o=rt(t, 16, 24), i=rt(t, 8, 16), z=S_IOTA[:]:
                        E.scalar_tensor_tensor(out=o, in0=i, scalar=16.0, in1=z,
                                               op0=ALU.mult, op1=ALU.subtract))
            for t in range(8):
                prog.op("vector", lambda E, o=rt(t, 25, 26), i=rt(t, 16, 24):
                        E.reduce_max(out=o, in_=i, axis=AX.X))
            for t in range(8):
                prog.op("vector", lambda E, o=rt(t, 8, 16), i=rt(t, 16, 24), v=rt(t, 25, 26):
                        E.tensor_scalar(out=o, in0=i, scalar1=v, scalar2=None, op0=ALU.is_ge))
            for t in range(8):
                prog.op("vector", lambda E, o=rt(t, 16, 24), i=rt(t, 8, 16), l=rt(t, 0, 8):
                        E.scalar_tensor_tensor(out=o, in0=i, scalar=NEG, in1=l,
                                               op0=ALU.mult, op1=ALU.add))
            for t in range(8):
                prog.op("vector", lambda E, o=rt(t, 26, 27), i=rt(t, 16, 24):
                        E.reduce_max(out=o, in_=i, axis=AX.X))
            for t in range(8):
                prog.op("vector", lambda E, o=rt(t, 27, 28), i=rt(t, 24, 25), j=rt(t, 26, 27):
                        E.tensor_tensor(out=o, in0=i, in1=j, op=ALU.subtract))
            prog.wait("vector", "gd", rq)
            gp = None
            for t in range(8):
                gp = prog.op("vector", lambda E, o=rt(t, 27, 28), i=rt(t, 27, 28),
                             r=RSTDC[:, t:t + 1]:
                             E.tensor_tensor(out=o, in0=i, in1=r, op=ALU.mult),
                             inc=("dv" if t == 7 else None))
            prog.wait("scalar", "dv", gp)
            sg = None
            for t in range(8):
                sg = prog.op("scalar", lambda E, o=rt(t, 28, 29), i=rt(t, 27, 28):
                             E.activation(out=o, in_=i, func=AF.Sigmoid), inc="ac")
            for t in (0, 1):
                sg = prog.op("scalar", lambda E, o=rt(t, 29, 30), i=rt(t, 28, 29):
                             E.activation(out=o, in_=i, func=AF.Sigmoid), inc="ac")
            prog.wait("vector", "ac", sg)
            se = {}
            for t in range(8):
                se[t] = prog.op("vector", lambda E, o=rt(t, 8, 16), i=rt(t, 8, 16),
                                w=rt(t, 28, 29):
                                E.tensor_scalar(out=o, in0=i, scalar1=w, scalar2=None,
                                                op0=ALU.mult), inc="dv")
            for ttile in range(8):
                prog.wait("tensor", "dv", se[ttile])
                banks.acquire(4)
                tr = prog.op("tensor", lambda E, o=PS[4][0:8, 0:128], i=rt(ttile, 8, 16):
                             E.transpose(out=o, in_=i, identity=S_IDF[:]), inc="pe")
                prog.wait("vector", "pe", tr)
                sel_done = prog.op("vector",
                                   lambda E, o=selT[0:8, ttile * 128:(ttile + 1) * 128],
                                   i=PS[4][0:8, 0:128]:
                                   E.tensor_copy(out=o, in_=i), inc="dv")
                banks.release(4, "dv", sel_done)

            state = {}
            for e in range(E1):
                prog.wait("gpsimd", "dv", sel_done)
                if ("rowb",) in state:
                    prog.wait("gpsimd", "pe", state[("rowb",)])
                rb = prog.dma("gpsimd", "gd", ROWB, selT[e:e + 1, :])
                prog.wait("tensor", "gd", rb)
                wb_pe = {}
                for half in (0, 1):
                    bank = 6 + half
                    banks.acquire(bank)
                    wb_pe[half] = prog.op(
                        "tensor",
                        lambda E, o=psf(bank), r=ROWB[0:1, half * 512:half * 512 + 512]:
                        E.matmul(out=o, lhsT=S_ONESF[:], rhs=r, start=True, stop=True), inc="pe")
                state[("rowb",)] = wb_pe[1]
                for half in (0, 1):
                    prog.wait("vector", "pe", wb_pe[half])
                    wv = prog.op("vector",
                                 lambda E, o=AR.f32(wb_o + half * 2048, 512), i=psf(6 + half):
                                 E.tensor_copy(out=o, in_=i), inc="dv")
                    banks.release(6 + half, "dv", wv)
                wb_dv = wv
                if DBG_ATTN:
                    prog.wait("gpsimd", "dv", wb_dv)
                    prog.dma("gpsimd", "gd", dbg_wb[e:e + 1, :], AR.f32(wb_o, T, rows=1))
                    prog.dma("gpsimd", "gd", dbg_wb[8 + e:9 + e, :],
                             AR.f32(wb_o, T, rows=1, row0=64))
                    if e == 0:
                        prog.dma("gpsimd", "gd", dbg_selt[:, :], selT)

                groups = [(2 * i, 2) for i in range(5)] + [(10, 1)]
                for gi, (ic0g, gn) in enumerate(groups):
                    buf = gi % 2
                    gwb = AR.bf(gw_o + buf * KCN * 256 * 2, KCN * 256)
                    uwb = AR.bf(uw_o + buf * KCN * 256 * 2, KCN * 256)
                    if ("gld_prev",) in state:
                        prog.wait("sync", "pe", state[("gld_prev",)])
                    srcg = bass.AP(mgw, e * D * INTER1 + ic0g * 128,
                                   [[INTER1, 128], [128 * INTER1, KCN], [1, gn * 128]])
                    srcu = bass.AP(muw, e * D * INTER1 + ic0g * 128,
                                   [[INTER1, 128], [128 * INTER1, KCN], [1, gn * 128]])
                    prog.dma("sync", "ld",
                             gwb.rearrange("p (kc n) -> p kc n", kc=KCN)[:, :, 0:gn * 128], srcg)
                    lu = prog.dma("sync", "ld",
                                  uwb.rearrange("p (kc n) -> p kc n", kc=KCN)[:, :, 0:gn * 128],
                                  srcu)
                    pe_last = None
                    for sub in range(gn):
                        ic = ic0g + sub
                        for half in (0, 1):
                            gbank = (2 * ic + half) % 2 * 2
                            ubank = gbank + 1
                            prog.wait("tensor", "ld", lu)
                            if e == 0 and ("moehwar",) not in state:
                                state[("moehwar",)] = 1
                            banks.acquire(gbank)
                            banks.acquire(ubank)
                            pc = None
                            for kc in range(KCN):
                                rhs = XBF[:, kc * T + half * 512: kc * T + half * 512 + 512]
                                prog.op("tensor",
                                        lambda E, o=psf(gbank),
                                        l=gwb[:, kc * 256 + sub * 128: kc * 256 + sub * 128 + 128],
                                        r=rhs, st=(kc == 0), sp=(kc == KCN - 1):
                                        E.matmul(out=o, lhsT=l, rhs=r, start=st, stop=sp))
                                pc = prog.op("tensor",
                                             lambda E, o=psf(ubank),
                                             l=uwb[:, kc * 256 + sub * 128: kc * 256 + sub * 128 + 128],
                                             r=rhs, st=(kc == 0), sp=(kc == KCN - 1):
                                             E.matmul(out=o, lhsT=l, rhs=r, start=st, stop=sp),
                                             inc=("pe" if kc == KCN - 1 else None))
                            pe_last = pc
                            prog.wait("scalar", "pe", pc)
                            gact = AR.f32(t1_o + (ic % 2) * 2048, 512)
                            sc_ = prog.op("scalar", lambda E, o=gact, i=psf(gbank):
                                          E.activation(out=o, in_=i, func=AF.Silu), inc="ac")
                            banks.release(gbank, "ac", sc_)
                            prog.wait("vector", "ac", sc_)
                            if half == 0 and sub == 0 and gi == 0:
                                prog.wait("vector", "dv", wb_dv)
                                if ("hwar", e) in state:
                                    prog.wait("vector", "pe", state[("hwar", e)])
                            prog.op("vector",
                                    lambda E, o=gact, i=gact,
                                    w=AR.f32(wb_o + half * 2048, 512):
                                    E.tensor_tensor(out=o, in0=i, in1=w, op=ALU.mult))
                            hdst = AR.bf(h_o + ic * 2048 + half * 1024, 512)
                            hv = prog.op("vector",
                                         lambda E, o=hdst, i=psf(ubank), g=gact:
                                         E.scalar_tensor_tensor(out=o, in0=i, scalar=1.0, in1=g,
                                                                op0=ALU.mult, op1=ALU.mult),
                                         inc="dv")
                            banks.release(ubank, "dv", hv)
                            state[("h", ic)] = hv
                    state[("gld_prev",)] = pe_last

                for hc in range(KCN):
                    buf = hc % 2
                    if ("dres",) in state:
                        prog.wait("sync", "dv", state[("dres",)])
                    srcd = bass.AP(mdw, e * INTER1 * D + hc * 128,
                                   [[D, 128], [128 * D, IC1], [1, 128]])
                    lc = prog.dma("sync", "ld",
                                  AR.bf(dwm_o + buf * IC1 * 128 * 2, IC1 * 128)
                                  .rearrange("p (ic n) -> p ic n", ic=IC1), srcd)
                    prog.wait("tensor", "ld", lc)
                    if hc == 0:
                        for ic in range(IC1):
                            prog.wait("tensor", "dv", state[("h", ic)])
                    for half in (0, 1):
                        bank = 4 + (2 * hc + half) % 2
                        banks.acquire(bank)
                        pc = None
                        for ic in range(IC1):
                            pc = prog.op("tensor",
                                         lambda E, o=psf(bank),
                                         l=AR.bf(dwm_o + buf * IC1 * 128 * 2 + ic * 256, 128),
                                         r=AR.bf(h_o + ic * 2048 + half * 1024, 512),
                                         st=(ic == 0), sp=(ic == IC1 - 1):
                                         E.matmul(out=o, lhsT=l, rhs=r, start=st, stop=sp),
                                         inc=("pe" if ic == IC1 - 1 else None))
                        prog.wait("vector", "pe", pc)
                        xsl = XT[:, hc * T + half * 512: hc * T + half * 512 + 512]
                        rc = prog.op("vector",
                                     lambda E, o=xsl, i=psf(bank), x=xsl:
                                     E.scalar_tensor_tensor(out=o, in0=i, scalar=0.0, in1=x,
                                                            op0=ALU.add, op1=ALU.add), inc="dv")
                        banks.release(bank, "dv", rc)
                        state[("dres",)] = rc
                    state[("dld", buf)] = pc
                state[("hwar", e + 1)] = pc
            return prog.sc["dv"]

        r = 0
        xbf_o, r = emit_ln(r)
        qt_o = None
        if upto not in ("ln1_0",):
            qt_o, qr = emit_qkv(0, xbf_o, r)
        if upto not in ("ln1_0", "qkv_0"):
            r = emit_attention(0, qt_o, qr)
            if upto != "attn_0":
                xbf_o, r = emit_ln(r)
                r = emit_ffn0(xbf_o)
                if upto != "ffn_0":
                    xbf_o, r = emit_ln(r)
                    qt_o, qr = emit_qkv(1, xbf_o, r)
                    r = emit_attention(1, qt_o, qr)
                    if upto != "attn_1":
                        xbf_o, r = emit_ln(r)
                        r = emit_moe(xbf_o)

        prog.wait("gpsimd", "dv", prog.sc["dv"])
        prog.wait("gpsimd", "ac", prog.sc["ac"])
        prog.wait("gpsimd", "pe", prog.sc["pe"])
        if upto == "ln1_0":
            prog.dma("gpsimd", "gd",
                     dbg_xbf.rearrange("(kc p) t -> p kc t", p=128),
                     AR.bf(0, KCN * T).rearrange("p (kc t) -> p kc t", kc=KCN))
            prog.dma("gpsimd", "gd", dbg_row[0:1, :], MR)
            prog.dma("gpsimd", "gd", dbg_row[1:2, :], RSTDR)
        if upto == "qkv_0":
            prog.wait("gpsimd", "cc0", 1)
            prog.dma("gpsimd", "gd",
                     dbg_qt.rearrange("(kc p) t -> p kc t", p=128),
                     AR.bf(qt_o, KCN * T).rearrange("p (kc t) -> p kc t", kc=KCN))
            prog.dma("gpsimd", "gd", dbg_ccin[:, :], cc_in[0].ap()[:, :])
            prog.dma("gpsimd", "gd", dbg_ccout[:, :], cc_out[0].ap()[0:4 * D, :])
        st_c = prog.dma("gpsimd", "gd",
                        xout.rearrange("(kc p) t -> p kc t", p=128),
                        XT[:].rearrange("p (kc t) -> p kc t", kc=KCN))
        prog.wait("gpsimd", "gd", st_c)

        with nc.Block() as block:
            @block.sync
            def _(E):
                for f in prog.ops["sync"]:
                    f(E)

            @block.vector
            def _(E):
                for f in prog.ops["vector"]:
                    f(E)

            @block.scalar
            def _(E):
                for f in prog.ops["scalar"]:
                    f(E)

            @block.tensor
            def _(E):
                for f in prog.ops["tensor"]:
                    f(E)

            @block.gpsimd
            def _(E):
                for f in prog.ops["gpsimd"]:
                    f(E)

    return nc


def _host_inputs(inputs):
    x = np.asarray(inputs["L0_attn_x"], dtype=np.float32)

    def cast(a):
        return np.ascontiguousarray(np.asarray(a, dtype=np.float32)).astype(BF)

    shared = {
        "wqkv0": cast(inputs["L0_attn_w_qkv"]),
        "wout0": cast(inputs["L0_attn_w_out"]),
        "gwT0": cast(np.asarray(inputs["L0_ffn_gate_w"], np.float32).T),
        "uwT0": cast(np.asarray(inputs["L0_ffn_up_w"], np.float32).T),
        "dwT0": cast(np.asarray(inputs["L0_ffn_down_w"], np.float32).T),
        "wqkv1": cast(inputs["L1_attn_w_qkv"]),
        "wout1": cast(inputs["L1_attn_w_proj"]),
        "mgw": cast(np.asarray(inputs["L1_ffn_gate_w"], np.float32).transpose(0, 2, 1)),
        "muw": cast(np.asarray(inputs["L1_ffn_up_w"], np.float32).transpose(0, 2, 1)),
        "mdw": cast(np.asarray(inputs["L1_ffn_down_w"], np.float32).transpose(0, 2, 1)),
        "c_id": np.eye(128, dtype=BF),
        "c_idf": np.eye(128, dtype=np.float32),
        "c_iota": np.broadcast_to(np.arange(8, dtype=np.float32), (128, 8)).copy(),
        "c_onesbf": np.ones((128, 1), dtype=BF),
        "c_onesf": np.ones((1, 128), dtype=np.float32),
        "c_gs": np.asarray(inputs["L0_ffn_gate_s"], np.float32).reshape(IC0, 128).T.copy(),
        "c_us": np.asarray(inputs["L0_ffn_up_s"], np.float32).reshape(IC0, 128).T.copy(),
        "c_ds": np.asarray(inputs["L0_ffn_down_s"], np.float32).reshape(KCN, 128).T.copy(),
        "c_bq": np.asarray(inputs["L1_attn_b_qkv"], np.float32)[0:2048].reshape(KCN, 128).T.copy(),
        "c_bv": np.asarray(inputs["L1_attn_b_qkv"], np.float32)[4096:6144].reshape(H, 128).T.copy(),
        "c_bp": np.asarray(inputs["L1_attn_b_proj"], np.float32).reshape(KCN, 128).T.copy(),
        "c_rwt": np.ascontiguousarray(
            np.asarray(inputs["L1_ffn_router_w"], np.float32).T.reshape(KCN, 128, E1)
            .transpose(1, 0, 2).reshape(128, KCN * E1)),
    }

    in_maps = []
    i = np.arange(128)[:, None]
    col = np.arange(512)[None, :]
    r_ = col // 64
    jj = col % 64
    for c in range(NCORES):
        xc = np.empty((D, T), dtype=np.float32)
        for s in range(2):
            xc[:, s * 512:(s + 1) * 512] = x[s][c::8, :].T
        mask = np.empty((128, 1024), dtype=np.float32)
        for kbrel in range(2):
            valid = (c + 8 * i) >= (r_ + 512 * kbrel + 8 * jj)
            mask[:, kbrel * 512:(kbrel + 1) * 512] = np.where(valid, 0.0, NEG)
        m = dict(shared)
        m["x"] = np.ascontiguousarray(xc)
        m["c_mask"] = mask.astype(BF)
        in_maps.append(m)
    return in_maps


_NC_CACHE = {}


def kernel(**inputs):
    in_maps = _host_inputs(inputs)
    if "all" not in _NC_CACHE:
        _NC_CACHE["all"] = build_program(upto="all")
    nc = _NC_CACHE["all"]
    res = run_bass_kernel_spmd(nc, in_maps, core_ids=list(range(NCORES)))
    x = np.asarray(inputs["L0_attn_x"], dtype=np.float32)
    out = np.empty_like(x)
    for c in range(NCORES):
        xc = res.results[c]["xout"]
        for s in range(2):
            out[s][c::8, :] = xc[:, s * 512:(s + 1) * 512].T
    return out
